# version 28
# baseline (speedup 1.0000x reference)
import numpy as np
from contextlib import ExitStack

import concourse.bass as bass
import concourse.mybir as mybir
from concourse.bass_utils import run_bass_kernel_spmd

F32 = mybir.dt.float32
BF16 = mybir.dt.bfloat16
I32 = mybir.dt.int32
AF = mybir.ActivationFunctionType
ALU = mybir.AluOpType
AX = mybir.AxisListType

S, D, DD, H, DK, DV, C = 4096, 1024, 2048, 16, 64, 128, 128
NCH = S // C
NG = 4
EPS_LN = 1e-5
EPS_Q = 1e-8
QMAX = 127.0
WGB_N = 5
SERIAL = False
SERIAL2 = False
SERIAL_VS = False
SERIAL_TV = False
SERIAL_TS = False
SERIAL_ATT = False
DBG = False
DETGATE = True


def _host_consts():
    gam = 1.0 - 2.0 ** (-5.0 - np.arange(H, dtype=np.float64))
    idx = np.arange(S, dtype=np.float64)
    ang = 1.0 / 10000.0 ** np.linspace(0.0, 1.0, DK // 2)
    th = idx[:, None] * ang[None, :]
    cos32 = np.cos(th).astype(np.float32)
    sin32 = np.sin(th).astype(np.float32)
    cosC = np.concatenate([cos32, cos32], 1)
    sinC = np.concatenate([-sin32, sin32], 1)
    j = np.arange(C, dtype=np.float64)
    qg = (gam[None, :] ** j[:, None]).astype(np.float32)
    kg = (0.125 * gam[None, :] ** (C - j)[:, None]).astype(np.float32)
    kg2 = (0.125 * gam[None, :] ** (-j)[:, None]).astype(np.float32)
    causal = (j[None, :] >= j[:, None]).astype(np.float32)
    rowsum = (1.0 - gam[None, :] ** (idx[:, None] + 1.0)) / (1.0 - gam[None, :])
    r = (rowsum ** -0.5).reshape(NCH, C, H).transpose(1, 0, 2).reshape(C, NCH * H).astype(np.float32)
    rinv = (rowsum ** 0.5).reshape(NCH, C, H).transpose(1, 0, 2).reshape(C, NCH * H).astype(np.float32)
    import ml_dtypes
    ident = np.eye(128, dtype=ml_dtypes.bfloat16)
    identf = np.eye(128, dtype=np.float32)
    ones_row = np.ones((1, 128), dtype=ml_dtypes.bfloat16)
    ones_col = np.ones((128, 1), dtype=ml_dtypes.bfloat16)
    return dict(cosC=cosC, sinC=sinC, qg=qg, kg=kg, kg2=kg2, causal=causal,
                r=r, rinv=rinv, gamC=(gam ** C).astype(np.float64),
                ident=ident, identf=identf, ones_row=ones_row, ones_col=ones_col)


_HC = _host_consts()


class _SelfSync:

    def __init__(self, eng, sem, ctr):
        self._e = eng
        self._s = sem
        self._c = ctr

    def __getattr__(self, name):
        f = getattr(self._e, name)
        if name == "wait_ge":
            return f

        def g(*a, **k):
            inst = f(*a, **k)
            inst.then_inc(self._s, 1)
            self._c[0] += 1
            return inst
        return g


class Plan:
    def __init__(self):
        self.count = {}
        self.phase = {}
        self.keys = []

    def add(self, key, sem_name, inc):
        assert key not in self.phase, key
        self.count[sem_name] = self.count.get(sem_name, 0) + inc
        self.phase[key] = (sem_name, self.count[sem_name])
        self.keys.append(key)


def build_nc():
    nc = bass.Bass("TRN2", target_bir_lowering=False, detect_race_conditions=False)

    x_h = nc.dram_tensor("x", [S, D], F32, kind="ExternalInput").ap()
    wq_h = nc.dram_tensor("wq", [D, D], F32, kind="ExternalInput").ap()
    wk_h = nc.dram_tensor("wk", [D, D], F32, kind="ExternalInput").ap()
    wv_h = nc.dram_tensor("wv", [D, DD], F32, kind="ExternalInput").ap()
    wg_h = nc.dram_tensor("wg", [D, DD], F32, kind="ExternalInput").ap()
    wo_h = nc.dram_tensor("wo", [DD, D], F32, kind="ExternalInput").ap()
    wf_h = nc.dram_tensor("wf", [D, D], F32, kind="ExternalInput").ap()
    bq_h = nc.dram_tensor("bq", [1, D], F32, kind="ExternalInput").ap()
    bk_h = nc.dram_tensor("bk", [1, D], F32, kind="ExternalInput").ap()
    bv_h = nc.dram_tensor("bv", [1, DD], F32, kind="ExternalInput").ap()
    bg_h = nc.dram_tensor("bg", [1, DD], F32, kind="ExternalInput").ap()
    bo_h = nc.dram_tensor("bo", [1, D], F32, kind="ExternalInput").ap()
    bf_h = nc.dram_tensor("bf", [1, D], F32, kind="ExternalInput").ap()
    cos_h = nc.dram_tensor("cosC", [S, DK], F32, kind="ExternalInput").ap()
    sin_h = nc.dram_tensor("sinC", [S, DK], F32, kind="ExternalInput").ap()
    qg_h = nc.dram_tensor("qg", [C, H], F32, kind="ExternalInput").ap()
    kg_h = nc.dram_tensor("kg", [C, H], F32, kind="ExternalInput").ap()
    kg2_h = nc.dram_tensor("kg2", [C, H], F32, kind="ExternalInput").ap()
    causal_h = nc.dram_tensor("causal", [C, C], F32, kind="ExternalInput").ap()
    r_h = nc.dram_tensor("r_t", [C, NCH * H], F32, kind="ExternalInput").ap()
    rinv_h = nc.dram_tensor("rinv_t", [C, NCH * H], F32, kind="ExternalInput").ap()
    ident_h = nc.dram_tensor("ident", [128, 128], BF16, kind="ExternalInput").ap()
    onesr_h = nc.dram_tensor("ones_row", [1, 128], BF16, kind="ExternalInput").ap()
    onesc_h = nc.dram_tensor("ones_col", [128, 1], BF16, kind="ExternalInput").ap()
    y_h = nc.dram_tensor("y", [S, D], F32, kind="ExternalOutput").ap()
    dbg_specs = []
    wg_sc = nc.dram_tensor("wg_scr", [D, DD], BF16, kind="Internal").ap()
    wv_sc = nc.dram_tensor("wv_scr", [D, DD], BF16, kind="Internal").ap()
    gamC = _HC["gamC"]

    with ExitStack() as st:
        def sb(name, shape, dt):
            return st.enter_context(nc.sbuf_tensor(name, shape, dt))

        wq = sb("wq_s", [128, 8 * D], BF16)
        wk = sb("wk_s", [128, 8 * D], BF16)
        wqT = sb("wqT_s", [128, 8 * D], BF16)
        wgb = [sb(f"wgb{i}", [128, DD], BF16) for i in range(WGB_N)]
        wo_r = sb("wo_r", [128, 16 * D], BF16)
        cosb = [sb(f"cosb{i}", [128, DK], F32) for i in range(2)]
        sinb = [sb(f"sinb{i}", [128, DK], F32) for i in range(2)]
        qg = sb("qg_s", [128, H], F32)
        kg = sb("kg_s", [128, H], F32)
        kg2 = sb("kg2_s", [128, H], F32)
        causal = sb("causal_s", [128, C], F32)
        r_t = sb("r_s", [128, NCH * H], F32)
        rinv_t = sb("rinv_s", [128, NCH * H], F32)
        ident = sb("ident_s", [128, 128], BF16)
        ones_row = sb("onesr_s", [1, 128], BF16)
        ones_col = sb("onesc_s", [128, 1], BF16)
        NBIAS = 2 * D + 2 * DD + 2 * D
        biasr = sb("biasr", [1, NBIAS], BF16)
        xb = [sb(f"xb{i}", [128, D], F32) for i in range(2)]
        centb = [sb(f"cent{i}", [128, D], F32) for i in range(2)]
        p1b = [sb(f"p1_{i}", [128, D], F32) for i in range(2)]
        cent = centb[0]
        p1 = p1b[0]
        p3 = sb("p3", [128, D // 2], I32)
        xn16 = sb("xn16", [128, D], BF16)
        xnT = sb("xnT", [128, 8 * 128], BF16)
        xqT = sb("xqT", [128, 8 * 128], BF16)
        q16 = sb("q16", [128, D], BF16)
        xq16 = q16
        k16 = xn16
        krgb = [sb(f"krg{i}", [128, D], BF16) for i in range(2)]
        qT = sb("qT", [64, H * 128], BF16)
        kT = sb("kT", [64, H * 128], BF16)
        vaug = sb("vaug", [128, H * 129], BF16)
        silu_g = sb("silu_g", [128, DD], BF16)
        stbuf = [sb(f"state{i}", [64, H * 129], F32) for i in range(2)]
        st_hi = sb("st_hi", [64, H * 129], BF16)
        st_lo = sb("st_lo", [64, H * 129], BF16)
        mskb = [sb(f"msk{i}", [128, 512], BF16) for i in range(2)]
        preT = silu_g
        prescrb = [sb(f"prescr{i}", [128, 512], BF16) for i in range(2)]
        prescr2b = [sb(f"prescr2{i}", [128, 512], BF16) for i in range(2)]
        cols = sb("cols", [128, 48], F32)
        padc = sb("padc", [128, 4], F32)
        gdump = sb("gdump", [128, 128], F32) if DBG else None
        colg = sb("colg", [128, 112], F32)

        if DBG:
            for nm, buf in [("d_xn16", xn16), ("d_xq16", xq16), ("d_q16", q16),
                            ("d_k16", k16), ("d_krg", krg), ("d_vaug", vaug),
                            ("d_qT", qT), ("d_kT", kT), ("d_msk", msk),
                            ("d_silu", silu_g), ("d_sthi", st_hi), ("d_stlo", st_lo),
                            ("d_preT", preT), ("d_prescr2", prescr2),
                            ("d_colg", colg), ("d_cols", cols), ("d_xnT", xnT),
                            ("d_xqT", xqT), ("d_wqT", wqT), ("d_st0", stbuf[0]),
                            ("d_st1", stbuf[1]), ("d_gdump", gdump)]:
                hshape = [buf.shape[0] if hasattr(buf, 'shape') else 128, 0]
                dbg_specs.append((nm, buf))
        dbg_aps = []
        for nm, buf in dbg_specs:
            bb = buf[:]
            dbg_aps.append((nc.dram_tensor(nm, [bb.shape[0], bb.shape[1]],
                                           bb.dtype, kind="ExternalOutput").ap(), buf))

        pq = st.enter_context(nc.psum_tensor("pq", [128, D], F32))
        pbig = st.enter_context(nc.psum_tensor("pbig", [128, DD], F32))
        ptall = st.enter_context(nc.psum_tensor("ptall", [128, 1024], BF16))
        psc = st.enter_context(nc.psum_tensor("psc", [128, 512], F32))

        s_x = st.enter_context(nc.semaphore("s_x"))
        s_w = st.enter_context(nc.semaphore("s_w"))
        s_q = st.enter_context(nc.semaphore("s_q"))
        s_T = st.enter_context(nc.semaphore("s_T"))
        s_V = st.enter_context(nc.semaphore("s_V"))
        s_S = st.enter_context(nc.semaphore("s_S"))
        s_y = st.enter_context(nc.semaphore("s_y"))
        s_vv = st.enter_context(nc.semaphore("s_vv"))
        s_ss = st.enter_context(nc.semaphore("s_ss"))
        s_wg = [st.enter_context(nc.semaphore(f"s_wg{i}")) for i in range(WGB_N)]
        SEMS = dict(s_x=s_x, s_w=s_w, s_T=s_T, s_V=s_V, s_S=s_S, s_y=s_y)
        for i in range(WGB_N):
            SEMS[f"s_wg{i}"] = s_wg[i]

        SIG0 = 1024 + 300
        DST0 = 1024

        plan = Plan()
        plan.add(("LDC",), "s_x", 16 * 12)
        plan.add(("LDW",), "s_w", 16 * 22)
        plan.add(("LDSC",), "s_w", 16 * 48)
        for kf in range(8):
            plan.add(("WFL", kf), "s_x", 16)
            plan.add(("WQZV", kf), "s_V", 1)
            plan.add(("WQZT", kf), "s_T", 1)
            plan.add(("WQZS", kf), "s_V", 1)
        plan.add(("VINIT",), "s_V", 1)
        plan.add(("VLN", 0), "s_V", 1)
        plan.add(("VQ", 0), "s_V", 1)
        plan.add(("TXN", 0), "s_T", 1)
        plan.add(("TXQ", 0), "s_T", 1)
        for c in range(NCH):
            plan.add(("X", c), "s_x", 16)
            plan.add(("TRG", c), "s_x", 32)
            for k in range(8):
                plan.add(("WV", c, k), f"s_wg{(c * 16 + k) % WGB_N}", 16)
            for k in range(8):
                plan.add(("WG", c, k), f"s_wg{(c * 16 + 8 + k) % WGB_N}", 16)
            plan.add(("SXN", c), "s_S", 1)
            plan.add(("SEVXNT", c), "s_S", 1)
            plan.add(("SEVXQ", c), "s_S", 1)
            plan.add(("TQ", c), "s_T", 1)
            plan.add(("VROPEQ", c), "s_V", 1)
            plan.add(("TK", c), "s_T", 1)
            plan.add(("VROPEK", c), "s_V", 1)
            if c >= 1:
                plan.add(("TSTU", c - 1, 0), "s_T", 1)
                for pr in range(8):
                    plan.add(("VSTU", c - 1, pr), "s_V", 1)
                plan.add(("VSTH", c - 1), "s_V", 1)
            for r_ in range(2):
                plan.add(("TTRQ", c, r_), "s_T", 1)
                plan.add(("SEVQ", c, r_), "s_S", 1)
            for r_ in range(2):
                plan.add(("TTRK", c, r_), "s_T", 1)
                plan.add(("SEVK", c, r_), "s_S", 1)
            for k in range(8):
                if c >= 1 and k < 7:
                    plan.add(("TSTU", c - 1, k + 1), "s_T", 1)
                plan.add(("TV", c, k), "s_T", 1)
            plan.add(("VEVV", c), "s_V", 1)
            for k in range(8):
                plan.add(("TG", c, k), "s_T", 1)
            plan.add(("SSILU", c), "s_S", 1)
            if DBG:
                plan.add(("VGD", c), "s_V", 1)
            for g in range(NG):
                plan.add(("TSC", c, g), "s_T", 1)
                plan.add(("TOUT", c, g), "s_T", 1)
            plan.add(("TTRP", c, 0), "s_T", 1)
            plan.add(("TTRP", c, 1), "s_T", 1)
            plan.add(("VMASK", c, 0), "s_V", 1)
            plan.add(("VMASK", c, 1), "s_V", 1)
            plan.add(("VCOMB", c, 0), "s_V", 1)
            plan.add(("VMASK", c, 2), "s_V", 1)
            plan.add(("VCOMB", c, 1), "s_V", 1)
            plan.add(("VPRE", c, 0), "s_V", 1)
            plan.add(("VMASK", c, 3), "s_V", 1)
            plan.add(("VCOMB", c, 2), "s_V", 1)
            plan.add(("VPRE", c, 1), "s_V", 1)
            plan.add(("VCOMB", c, 3), "s_V", 1)
            plan.add(("VPRE", c, 2), "s_V", 1)
            plan.add(("VPRE", c, 3), "s_V", 1)
            plan.add(("SSQ", c, 0), "s_S", 1)
            plan.add(("SSQ", c, 1), "s_S", 1)
            plan.add(("SAPPLY", c, 0), "s_S", 1)
            plan.add(("SSQ", c, 2), "s_S", 1)
            plan.add(("SAPPLY", c, 1), "s_S", 1)
            plan.add(("SSQ", c, 3), "s_S", 1)
            plan.add(("SEVPRE", c, 0), "s_S", 1)
            plan.add(("SAPPLY", c, 2), "s_S", 1)
            plan.add(("SEVPRE", c, 1), "s_S", 1)
            plan.add(("SAPPLY", c, 3), "s_S", 1)
            plan.add(("SEVPRE", c, 2), "s_S", 1)
            plan.add(("SEVPRE", c, 3), "s_S", 1)
            plan.add(("TTRP", c, 2), "s_T", 1)
            plan.add(("TTRP", c, 3), "s_T", 1)
            if c + 1 < NCH:
                plan.add(("VLN", c + 1), "s_V", 1)
                plan.add(("VQ", c + 1), "s_V", 1)
            plan.add(("TFFN", c), "s_T", 1)
            for k in range(16):
                plan.add(("TWO", c, k), "s_T", 1)
            if c + 1 < NCH:
                plan.add(("TXN", c + 1), "s_T", 1)
                plan.add(("TXQ", c + 1), "s_T", 1)
            plan.add(("VRES", c), "s_V", 1)
            plan.add(("VFIN", c), "s_V", 1)
            if c >= 1:
                plan.add(("Y", c - 1), "s_y", 16)
        plan.add(("Y", NCH - 1), "s_y", 16)
        if DBG:
            plan.add(("DBGD",), "s_y", 16 * len(dbg_specs))

        DEPS = {}

        def dd(key, *deps):
            DEPS[key] = [d for d in deps if d is not None]

        def pv(name, c, *rest):
            return (name, c, *rest) if c >= 0 else None

        dd(("LDC",))
        dd(("LDW",))
        dd(("LDSC",))
        for kf in range(8):
            dd(("WFL", kf), ("WQZV", kf - 1) if kf else None)
            dd(("WQZV", kf), ("WFL", kf), ("WQZS", kf - 1) if kf else None)
            dd(("WQZT", kf), ("WQZV", kf), ("LDC",),
               ("WQZS", kf - 1) if kf else None)
            dd(("WQZS", kf), ("WQZT", kf))
        dd(("VINIT",), ("WQZS", 7))
        if DBG:
            dd(("DBGD",), ("VFIN", NCH - 1), ("Y", NCH - 1))
        for c in range(NCH):
            dd(("X", c), pv("Y", c - 2))
            dd(("TRG", c), pv("VROPEK", c - 2))
            for k in range(8):
                dd(("WV", c, k),
                   ("TV", c, k - WGB_N) if k >= WGB_N else
                   (pv("TG", c - 1, k + 8 - WGB_N) if c >= 1 else ("LDSC",)),
                   ("TV", c, k - 1) if (SERIAL2 and k >= 1) else None)
            for k in range(8):
                dd(("WG", c, k),
                   ("TG", c, k - WGB_N) if k >= WGB_N else ("TV", c, k + 8 - WGB_N),
                   ("TG", c, k - 1) if (SERIAL2 and k >= 1) else None)
            dd(("VLN", c), ("X", c), ("LDC",), pv("SSQ", c - 1, 3))
            dd(("SXN", c), ("VLN", c), pv("TTRK", c - 1, 1))
            dd(("VQ", c), ("VLN", c), pv("TXQ", c - 1), ("VINIT",))
            dd(("TXN", c), ("SXN", c), pv("SEVPRE", c - 1, 3), ("WQZS", 7))
            dd(("SEVXNT", c), ("TXN", c), pv("TG", c - 1, 7))
            dd(("TXQ", c), ("VQ", c), ("SEVXNT", c))
            dd(("SEVXQ", c), ("TXQ", c), pv("TFFN", c - 1))
            dd(("TQ", c), ("SEVXNT", c), ("LDW",), pv("VRES", c - 1), ("WQZS", 7))
            dd(("VROPEQ", c), ("TQ", c), ("TRG", c), ("VQ", c),
               pv("TTRQ", c - 1, 1))
            dd(("TK", c), ("SEVXNT", c), ("LDW",), pv("VFIN", c - 1), ("WQZS", 7))
            dd(("VROPEK", c), ("TK", c), pv("TTRK", c - 1, 1))
            dd(("TTRQ", c, 0), ("VROPEQ", c), ("SEVXQ", c), pv("TOUT", c - 1, 3))
            dd(("SEVQ", c, 0), ("TTRQ", c, 0))
            dd(("TTRQ", c, 1), ("SEVQ", c, 0))
            dd(("SEVQ", c, 1), ("TTRQ", c, 1))
            dd(("TTRK", c, 0), ("VROPEK", c), ("SEVQ", c, 1))
            dd(("SEVK", c, 0), ("TTRK", c, 0))
            dd(("TTRK", c, 1), ("SEVK", c, 0))
            dd(("SEVK", c, 1), ("TTRK", c, 1))
            for k in range(8):
                dd(("TV", c, k), ("WV", c, k), ("LDW",),
                   ("VROPEK", c) if k == 0 else ("TV", c, k - 1),
                   pv("VFIN", c - 1) if k == 0 else None,
                   pv("VCOMB", c - 1, 3) if k == 0 else None)
            dd(("VEVV", c), ("TV", c, 7), pv("TSTU", c - 1, 7), ("VINIT",))
            for k in range(8):
                dd(("TG", c, k),
                   ("VEVV", c) if k == 0 else ("TG", c, k - 1),
                   ("WG", c, k), ("LDW",),
                   pv("VPRE", c - 1, 3) if k == 0 else None)
            dd(("SSILU", c), ("TG", c, 7))
            if DBG:
                dd(("VGD", c), ("SSILU", c))
            for g in range(NG):
                dd(("TSC", c, g), ("SEVQ", c, 1), ("SEVK", c, 1),
                   ("VMASK", c, g - 1) if g >= 1 else
                   (pv("VMASK", c - 1, 3) if c >= 1 else ("WQZS", 7)))
                dd(("VMASK", c, g), ("TSC", c, g), ("LDC",),
                   ("TOUT", c, g - 2) if g >= 2 else pv("TOUT", c - 1, g + 2))
                dd(("TOUT", c, g), ("VMASK", c, g), ("VEVV", c), ("SSILU", c),
                   ("VGD", c) if DBG else None,
                   pv("VSTH", c - 1),
                   ("SSQ", c, g - 3) if g >= 3 else pv("SSQ", c - 1, g + 1),
                   ("SAPPLY", c, g - 3) if g >= 3 else pv("SAPPLY", c - 1, g + 1))
                dd(("SSQ", c, g), ("TOUT", c, g),
                   ("VPRE", c, g - 2) if g >= 2 else pv("VPRE", c - 1, g + 2))
                dd(("VCOMB", c, g), ("TOUT", c, g), ("SSQ", c, g))
                dd(("SAPPLY", c, g), ("VCOMB", c, g), ("SSILU", c),
                   ("VPRE", c, g - 2) if g >= 2 else pv("VPRE", c - 1, g + 2))
                dd(("VPRE", c, g), ("SAPPLY", c, g),
                   ("TTRP", c, g - 2) if g >= 2 else pv("TTRP", c - 1, g + 2))
                dd(("TTRP", c, g), ("VPRE", c, g), ("SEVK", c, 1),
                   ("SEVPRE", c, g - 1) if g >= 1 else None)
                dd(("SEVPRE", c, g), ("TTRP", c, g), pv("TWO", c - 1, 15))
            for pr in range(8):
                dd(("TSTU", c, pr), ("VROPEK", c), ("VEVV", c),
                   ("VROPEQ", c + 1) if c + 1 < NCH else None,
                   ("VSTU", c, pr - 1) if pr >= 1 else None)
                dd(("VSTU", c, pr), ("TSTU", c, pr),
                   pv("VSTH", c - 1))
            dd(("VSTH", c), ("VSTU", c, 7), ("TOUT", c, 3))
            dd(("TFFN", c), ("SEVXQ", c), ("SAPPLY", c, 3), ("SAPPLY", c, 2), ("WQZS", 7))
            for k in range(16):
                dd(("TWO", c, k), ("SEVPRE", c, k // 4), ("LDSC",), ("LDW",),
                   ("VROPEQ", c) if k == 0 else None,
                   pv("VSTU", c - 1, 7) if k == 0 else None)
            dd(("VRES", c), ("TWO", c, 15))
            dd(("VFIN", c), ("VRES", c), ("TFFN", c), ("SXN", c))
            dd(("Y", c), ("VFIN", c))

        if SERIAL:
            comp = [k for k in plan.keys
                    if plan.phase[k][0] in ("s_T", "s_V", "s_S")]
            for i in range(1, len(comp)):
                DEPS.setdefault(comp[i], []).append(comp[i - 1])
        elif SERIAL_VS or SERIAL_TV or SERIAL_TS:
            sel = ("s_V", "s_S") if SERIAL_VS else (("s_T", "s_V") if SERIAL_TV else ("s_T", "s_S"))
            comp = [k for k in plan.keys if plan.phase[k][0] in sel]
            for i in range(1, len(comp)):
                DEPS.setdefault(comp[i], []).append(comp[i - 1])
        elif SERIAL_ATT:
            names = ("VLN", "SXN", "VQ", "TXN", "SEVXNT", "TXQ", "SEVXQ",
                     "TQ", "VROPEQ", "TK", "VROPEK", "TTRQ", "SEVQ", "TTRK",
                     "SEVK", "TV", "VEVV", "TG", "SSILU", "TFFN", "TWO", "VFIN")
            comp = [k for k in plan.keys if k[0] in names]
            for i in range(1, len(comp)):
                DEPS.setdefault(comp[i], []).append(comp[i - 1])

        EFOR = {"s_V": "V", "s_S": "S", "s_T": "T", "s_x": "SY", "s_y": "SY",
                "s_w": "G"}
        for i in range(WGB_N):
            EFOR[f"s_wg{i}"] = "G"
        edges = {k: list(ds) for k, ds in DEPS.items()}
        prev_by_eng = {}
        for k in plan.keys:
            eng = EFOR[plan.phase[k][0]]
            if eng in prev_by_eng:
                edges.setdefault(k, []).append(prev_by_eng[eng])
            prev_by_eng[eng] = k
        color = {}

        def _cyc(node, path):
            color[node] = 1
            path.append(node)
            for dep_n in edges.get(node, []):
                st_ = color.get(dep_n, 0)
                if st_ == 1:
                    i0 = path.index(dep_n)
                    raise AssertionError(f"dependency cycle: {path[i0:] + [dep_n]}")
                if st_ == 0:
                    _cyc(dep_n, path)
            path.pop()
            color[node] = 2
        import sys as _sys
        _sys.setrecursionlimit(100000)
        for k in plan.keys:
            if color.get(k, 0) == 0:
                _cyc(k, [])

        def waits(eng, key):
            my_sem = plan.phase[key][0]
            best = {}
            for dep_key in DEPS.get(key, []):
                sname, cnt = plan.phase[dep_key]
                if sname == my_sem:
                    continue
                if best.get(sname, -1) < cnt:
                    best[sname] = cnt
            for sname, cnt in best.items():
                eng.wait_ge(SEMS[sname], cnt)

        def fin(inst, key):
            sname, _ = plan.phase[key]
            inc = 16 if sname.startswith(("s_x", "s_w", "s_y")) else 1
            inst.then_inc(SEMS[sname], inc)
            return inst

        def mkfin(padfn):
            def f(inst, key):
                padfn()
                last = padfn()
                sname, _ = plan.phase[key]
                inc = 16 if sname.startswith(("s_x", "s_w", "s_y")) else 1
                last.then_inc(SEMS[sname], inc)
                return last
            return f

        build_nc.plan = plan
        block = st.enter_context(nc.Block())

        @block.sync
        def _(sync):
            sxc = [0]
            syc = [0]

            def xgate():
                if DETGATE:
                    sync.wait_ge(s_x, sxc[0])
                sxc[0] += 16

            def ygate():
                if DETGATE:
                    sync.wait_ge(s_y, syc[0])
                syc[0] += 16

            for dst, src in [(qg[:], qg_h), (kg[:], kg_h), (kg2[:], kg2_h),
                             (causal[:], causal_h), (r_t[:], r_h),
                             (rinv_t[:], rinv_h), (ident[:], ident_h),
                             (ones_row[:], onesr_h),
                             (ones_col[:], onesc_h), (cosb[1][:], cos_h[0:C, :]),
                             (sinb[1][:], sin_h[0:C, :]),
                             (cosb[0][:], cos_h[0:C, :])]:
                xgate()
                sync.dma_start(dst, src).then_inc(s_x, 16)
            for kf in range(8):
                key = ("WFL", kf)
                waits(sync, key)
                xgate()
                fin(sync.dma_start(p1[:], wf_h[kf * 128:(kf + 1) * 128, :]), key)
            def emit_y(cv):
                key = ("Y", cv)
                waits(sync, key)
                ygate()
                fin(sync.dma_start(y_h[cv * C:(cv + 1) * C, :], xb[cv % 2][:]), key)

            for c in range(NCH):
                key = ("X", c)
                waits(sync, key)
                xgate()
                fin(sync.dma_start(xb[c % 2][:], x_h[c * C:(c + 1) * C, :]), key)
                key = ("TRG", c)
                waits(sync, key)
                xgate()
                sync.dma_start(cosb[c % 2][:], cos_h[c * C:(c + 1) * C, :]).then_inc(s_x, 16)
                xgate()
                sync.dma_start(sinb[c % 2][:], sin_h[c * C:(c + 1) * C, :]).then_inc(s_x, 16)
                if c >= 1:
                    emit_y(c - 1)
            emit_y(NCH - 1)
            if DBG:
                key = ("DBGD",)
                waits(sync, key)
                sync.wait_ge(s_T, plan.count["s_T"])
                sync.wait_ge(s_V, plan.count["s_V"])
                sync.wait_ge(s_S, plan.count["s_S"])
                for ap_, buf in dbg_aps:
                    sync.dma_start(ap_, buf[:]).then_inc(s_y, 16)
            sync.wait_ge(s_y, plan.count["s_y"])

        @block.gpsimd
        def _(g):
            gwc = [0]

            def wgate():
                if DETGATE:
                    g.wait_ge(s_w, gwc[0])
                gwc[0] += 16

            key = ("LDW",)
            for k in range(8):
                wgate()
                g.dma_start(wq[:, k * D:(k + 1) * D], wq_h[k * 128:(k + 1) * 128, :]).then_inc(s_w, 16)
            for k in range(8):
                wgate()
                g.dma_start(wk[:, k * D:(k + 1) * D], wk_h[k * 128:(k + 1) * 128, :]).then_inc(s_w, 16)
            for bslc, bh in [(biasr[0:1, 0:D], bq_h), (biasr[0:1, D:2 * D], bk_h),
                             (biasr[0:1, 2 * D:2 * D + DD], bv_h),
                             (biasr[0:1, 2 * D + DD:2 * D + 2 * DD], bg_h),
                             (biasr[0:1, 2 * D + 2 * DD:3 * D + 2 * DD], bo_h),
                             (biasr[0:1, 3 * D + 2 * DD:4 * D + 2 * DD], bf_h)]:
                wgate()
                g.dma_start(bslc, bh).then_inc(s_w, 16)
            key = ("LDSC",)
            base = plan.phase[("LDW",)][1]
            n = 0
            for k in range(8):
                wgate()
                g.dma_start(wgb[k % 2][:], wv_h[k * 128:(k + 1) * 128, :]).then_inc(s_w, 16)
                n += 1
                g.wait_ge(s_w, base + n * 16)
                wgate()
                g.dma_start(wv_sc[k * 128:(k + 1) * 128, :], wgb[k % 2][:]).then_inc(s_w, 16)
                n += 1
            for k in range(8):
                wgate()
                g.dma_start(wgb[k % 2][:], wg_h[k * 128:(k + 1) * 128, :]).then_inc(s_w, 16)
                n += 1
                g.wait_ge(s_w, base + n * 16)
                wgate()
                g.dma_start(wg_sc[k * 128:(k + 1) * 128, :], wgb[k % 2][:]).then_inc(s_w, 16)
                n += 1
            for k in range(16):
                wgate()
                inst = g.dma_start(wo_r[:, k * D:(k + 1) * D],
                                   wo_h[k * 128:(k + 1) * 128, :])
                if k < 15:
                    inst.then_inc(s_w, 16)
                    n += 1
            fin(inst, key)
            for c in range(NCH):
                for k in range(8):
                    key = ("WV", c, k)
                    waits(g, key)
                    fin(g.dma_start(wgb[(c * 16 + k) % WGB_N][:],
                                    wv_sc[k * 128:(k + 1) * 128, :]), key)
                for k in range(8):
                    key = ("WG", c, k)
                    waits(g, key)
                    fin(g.dma_start(wgb[(c * 16 + 8 + k) % WGB_N][:],
                                    wg_sc[k * 128:(k + 1) * 128, :]), key)


        def bc_heads(handle, width, offset=0):
            return bass.AP(handle, offset, [[DK, 128], [0, H], [1, width]])

        def bc_heads8(handle, width, offset=0):
            return bass.AP(handle, offset, [[DK, 128], [0, 8], [1, width]])

        def bc4(handle):
            return bass.AP(handle, 0, [[C, 128], [0, 4], [1, C]])

        OBOFF = (0, 512, 1536)

        def ob_off(c, g):
            return OBOFF[(4 * c + g) % 3]

        def ob_ap(c, g):
            o = ob_off(c, g)
            return pbig[:, o:o + 512]

        MAXN = -3.0e38
        MINP = 3.0e38

        @block.vector
        def _(rvec):
            vvc = [0]
            vec = _SelfSync(rvec, s_vv, vvc)

            def pad():
                rvec.wait_ge(s_vv, vvc[0])

            def vfin(inst, key):
                rvec.wait_ge(s_vv, vvc[0])
                return fin(rvec.nop(), key)

            for kf in range(8):
                key = ("WQZV", kf)
                waits(vec, key)
                vec.reduce_max(cols[:, 9:10], p1[:], axis=AX.X, apply_absolute_value=True)
                pad()
                vec.tensor_scalar(cols[:, 10:11], cols[:, 9:10], EPS_Q, MINP, ALU.max, ALU.min)
                pad()
                vec.tensor_scalar(cols[:, 12 + kf:13 + kf], cols[:, 10:11], 1.0 / QMAX, MAXN, ALU.mult, ALU.max)
                vec.reciprocal(cols[:, 11:12], cols[:, 10:11])
                pad()
                vec.tensor_scalar(cols[:, 21:22], cols[:, 11:12], QMAX, MAXN, ALU.mult, ALU.max)
                pad()
                vec.tensor_scalar_mul(cent[:], p1[:], cols[:, 21:22])
                pad()
                vec.tensor_copy(p3[:], cent[:, 0:512])
                pad()
                vec.tensor_copy(cent[:, 0:512], p3[:])
                pad()
                vec.tensor_copy(p3[:], cent[:, 512:1024])
                pad()
                vec.tensor_copy(cent[:, 512:1024], p3[:])
                pad()
                vec.tensor_scalar_mul(p1[:], cent[:], cols[:, 12 + kf:13 + kf])
                pad()
                vfin(vec.tensor_copy(xq16[:], p1[:]), key)
                key = ("WQZS", kf)
                waits(vec, key)
                inst = None
                for bb in range(8):
                    inst = vec.tensor_copy(wqT[:, bb * D + kf * 128:bb * D + kf * 128 + 128],
                                           ptall[:, bb * 128:(bb + 1) * 128])
                vfin(inst, key)
            key = ("VINIT",)
            waits(vec, key)
            vec.memset(stbuf[0][:], 0.0)
            vec.memset(stbuf[1][:], 0.0)
            vec.memset(st_hi[:], 0.0)
            vec.memset(st_lo[:], 0.0)
            inst = None
            for h in range(H):
                inst = vec.memset(vaug[:, h * 129 + 128:h * 129 + 129], 1.0)
            vfin(inst, key)

            def emit_vres(cv):
                xcv = xb[cv % 2]
                cbv = centb[cv % 2]
                key = ("VRES", cv)
                waits(vec, key)
                vec.tensor_tensor(cbv[:, 0:512], xcv[:, 0:512], pq[:, 0:512], ALU.add)
                vfin(vec.tensor_tensor(cbv[:, 512:1024], xcv[:, 512:1024],
                                       pq[:, 512:1024], ALU.add), key)

            def emit_vfin(cv):
                xcv = xb[cv % 2]
                cbv = centb[cv % 2]
                ppv = p1b[cv % 2]
                OV = 24 + 12 * (cv % 2)
                key = ("VFIN", cv)
                waits(vec, key)
                vec.scalar_tensor_tensor(cols[:, OV + 8:OV + 9], cols[:, OV + 5:OV + 6], 1.0,
                                         cols[:, OV + 11:OV + 12], ALU.mult, ALU.mult)
                pad()
                vec.tensor_scalar(cols[:, OV + 9:OV + 10], cols[:, OV + 8:OV + 9],
                                  EPS_Q, MINP, ALU.max, ALU.min)
                pad()
                vec.tensor_scalar(cols[:, OV + 10:OV + 11], cols[:, OV + 9:OV + 10],
                                  1.0 / QMAX, MAXN, ALU.mult, ALU.max)
                pad()
                vec.tensor_scalar_mul(ppv[:, 0:512], pbig[:, 0:512], cols[:, OV + 10:OV + 11])
                vec.tensor_scalar_mul(ppv[:, 512:1024], pbig[:, 512:1024], cols[:, OV + 10:OV + 11])
                pad()
                vfin(vec.tensor_add(xcv[:], cbv[:], ppv[:]), key)

            def emit_vln(cv):
                xcv = xb[cv % 2]
                cbv = centb[cv % 2]
                ppv = p1b[cv % 2]
                OV = 24 + 12 * (cv % 2)
                key = ("VLN", cv)
                waits(vec, key)
                vec.reduce_sum(cols[:, OV + 0:OV + 1], xcv[:], axis=AX.X)
                pad()
                vec.tensor_scalar(cols[:, OV + 1:OV + 2], cols[:, OV + 0:OV + 1],
                                  -1.0 / D, MAXN, ALU.mult, ALU.max)
                pad()
                vec.tensor_scalar_add(cbv[:], xcv[:], cols[:, OV + 1:OV + 2])
                pad()
                vec.scalar_tensor_tensor(ppv[:], cbv[:], 1.0, cbv[:], ALU.mult, ALU.mult)
                pad()
                vec.reduce_sum(cols[:, OV + 2:OV + 3], ppv[:], axis=AX.X)
                pad()
                vec.tensor_scalar(cols[:, OV + 3:OV + 4], cols[:, OV + 2:OV + 3],
                                  1.0 / D, EPS_LN, ALU.mult, ALU.add)
                pad()
                vfin(vec.reciprocal(cols[:, OV + 4:OV + 5], cols[:, OV + 3:OV + 4]), key)

            def emit_vq(cv):
                cbv = centb[cv % 2]
                ppv = p1b[cv % 2]
                OV = 24 + 12 * (cv % 2)
                key = ("VQ", cv)
                waits(vec, key)
                vec.reduce_max(cols[:, OV + 5:OV + 6], cbv[:], axis=AX.X, apply_absolute_value=True)
                pad()
                vec.reciprocal(cols[:, OV + 6:OV + 7], cols[:, OV + 5:OV + 6])
                pad()
                vec.tensor_scalar(cols[:, OV + 7:OV + 8], cols[:, OV + 6:OV + 7],
                                  QMAX, MAXN, ALU.mult, ALU.max)
                pad()
                vec.tensor_scalar_mul(ppv[:], cbv[:], cols[:, OV + 7:OV + 8])
                pad()
                vec.tensor_copy(p3[:], ppv[:, 0:512])
                pad()
                vec.tensor_copy(ppv[:, 0:512], p3[:])
                pad()
                vec.tensor_copy(p3[:], ppv[:, 512:1024])
                pad()
                vec.tensor_copy(ppv[:, 512:1024], p3[:])
                pad()
                vfin(vec.tensor_copy(xq16[:], ppv[:]), key)

            def emit_vstu(cv, pr):
                scv = stbuf[cv % 2]
                sdv = stbuf[(cv + 1) % 2]
                key = ("VSTU", cv, pr)
                waits(vec, key)
                inst = None
                for hh in range(2):
                    h = pr * 2 + hh
                    inst = vec.scalar_tensor_tensor(
                        sdv[0:64, h * 129:(h + 1) * 129],
                        scv[0:64, h * 129:(h + 1) * 129], float(gamC[h]),
                        pq[0:64, hh * 129:(hh + 1) * 129],
                        ALU.mult, ALU.add)
                vfin(inst, key)

            def emit_vsth(cv):
                scv = stbuf[cv % 2]
                sdv = stbuf[(cv + 1) % 2]
                key = ("VSTH", cv)
                waits(vec, key)
                pad()
                vec.tensor_copy(st_hi[:], sdv[:])
                pad()
                vec.tensor_copy(scv[:], st_hi[:])
                pad()
                vfin(vec.tensor_sub(st_lo[:], sdv[:], scv[:]), key)

            emit_vln(0)
            emit_vq(0)
            for c in range(NCH):
                xcur = xb[c % 2]
                cb = centb[c % 2]
                pp = p1b[c % 2]
                CO = 24 + 12 * (c % 2)
                cosc, sinc = cosb[c % 2], sinb[c % 2]
                for which, key_name in [(0, "VROPEQ"), (1, "VROPEK")]:
                    key = (key_name, c)
                    waits(vec, key)
                    psrc = pq if which == 0 else pbig
                    for hf in range(2):
                        sl = slice(hf * 512, (hf + 1) * 512)
                        pq3 = psrc[:, sl].rearrange("p (h t) -> p h t", t=DK)
                        p13 = pp[:, sl].rearrange("p (h t) -> p h t", t=DK)
                        c3 = cb[:, sl].rearrange("p (h t) -> p h t", t=DK)
                        vec.tensor_tensor(p13[:, :, 0:32], pq3[:, :, 32:64],
                                          bc_heads8(sinc, 32, 0), ALU.mult)
                        vec.tensor_tensor(p13[:, :, 32:64], pq3[:, :, 0:32],
                                          bc_heads8(sinc, 32, 32), ALU.mult)
                        vec.tensor_tensor(c3, pq3, bc_heads8(cosc, DK, 0), ALU.mult)
                    pad()
                    vec.tensor_add(silu_g[:, 0:1024], pp[:], cb[:])
                    pad()
                    if which == 0:
                        inst = None
                        for h in range(H):
                            inst = vec.tensor_scalar_mul(
                                q16[:, h * DK:(h + 1) * DK],
                                silu_g[:, h * DK:(h + 1) * DK], qg[:, h:h + 1])
                        vfin(inst, key)
                    else:
                        for h in range(H):
                            vec.tensor_scalar_mul(
                                k16[:, h * DK:(h + 1) * DK],
                                silu_g[:, h * DK:(h + 1) * DK], kg2[:, h:h + 1])
                        inst = None
                        for h in range(H):
                            inst = vec.tensor_scalar_mul(
                                krgb[c % 2][:, h * DK:(h + 1) * DK],
                                silu_g[:, h * DK:(h + 1) * DK], kg[:, h:h + 1])
                        vfin(inst, key)
                if c >= 1:
                    for pr in range(8):
                        emit_vstu(c - 1, pr)
                    emit_vsth(c - 1)
                key = ("VEVV", c)
                waits(vec, key)
                inst = None
                for h in range(H):
                    inst = vec.tensor_copy(vaug[:, h * 129:h * 129 + 128],
                                           pbig[:, h * 128:(h + 1) * 128])
                vfin(inst, key)
                if DBG:
                    key = ("VGD", c)
                    waits(vec, key)
                    vfin(vec.tensor_copy(gdump[:], pbig[:, 0:128]), key)
                def emit_vmask(g):
                    key = ("VMASK", c, g)
                    waits(vec, key)
                    vfin(vec.tensor_tensor(
                        mskb[g % 2][:].rearrange("p (h t) -> p h t", t=128),
                        psc[:].rearrange("p (h t) -> p h t", t=128),
                        bc4(causal), ALU.mult), key)

                def emit_vcomb(g):
                    key = ("VCOMB", c, g)
                    waits(vec, key)
                    G = 56 * (g % 2)
                    rsl = r_t[:, c * H + g * 4:c * H + g * 4 + 4]
                    risl = rinv_t[:, c * H + g * 4:c * H + g * 4 + 4]
                    vec.tensor_copy(colg[:, G + 12:G + 16],
                                    pbig[:, SIG0 + g * 4:SIG0 + g * 4 + 4])
                    pad()
                    vec.scalar_tensor_tensor(colg[:, G + 16:G + 20], colg[:, G + 12:G + 16],
                                             1.0, rsl, ALU.mult, ALU.mult)
                    vec.tensor_scalar(colg[:, G + 40:G + 44], colg[:, G + 0:G + 4],
                                      -1.0 / DV, MAXN, ALU.mult, ALU.max)
                    vec.tensor_scalar(colg[:, G + 48:G + 52], colg[:, G + 44:G + 48],
                                      1.0 / DV, MAXN, ALU.mult, ALU.max)
                    pad()
                    vec.tensor_scalar(colg[:, G + 20:G + 24], colg[:, G + 16:G + 20],
                                      -1.0, MAXN, ALU.mult, ALU.max)
                    vec.scalar_tensor_tensor(colg[:, G + 4:G + 8], colg[:, G + 40:G + 44],
                                             1.0, colg[:, G + 40:G + 44], ALU.mult, ALU.mult)
                    pad()
                    vec.scalar_tensor_tensor(colg[:, G + 24:G + 28], colg[:, G + 16:G + 20],
                                             1.0, colg[:, G + 20:G + 24], ALU.mult, ALU.max)
                    vec.scalar_tensor_tensor(colg[:, G + 52:G + 56], colg[:, G + 48:G + 52],
                                             1.0, colg[:, G + 4:G + 8], ALU.mult, ALU.subtract)
                    pad()
                    vec.tensor_scalar(colg[:, G + 28:G + 32], colg[:, G + 24:G + 28],
                                      1.0, MINP, ALU.max, ALU.min)
                    vec.tensor_scalar(colg[:, G + 44:G + 48], colg[:, G + 52:G + 56],
                                      0.0, MINP, ALU.max, ALU.min)
                    pad()
                    vec.scalar_tensor_tensor(colg[:, G + 32:G + 36], colg[:, G + 28:G + 32],
                                             1.0, risl, ALU.mult, ALU.mult)
                    pad()
                    vec.scalar_tensor_tensor(colg[:, G + 36:G + 40], colg[:, G + 32:G + 36],
                                             EPS_LN, colg[:, G + 32:G + 36], ALU.mult, ALU.mult)
                    pad()
                    vec.scalar_tensor_tensor(colg[:, G + 48:G + 52], colg[:, G + 44:G + 48],
                                             1.0, colg[:, G + 36:G + 40], ALU.mult, ALU.add)
                    pad()
                    vfin(vec.reciprocal(colg[:, G + 52:G + 56], colg[:, G + 48:G + 52]), key)

                def emit_vpre(g):
                    key = ("VPRE", c, g)
                    waits(vec, key)
                    G = 56 * (g % 2)
                    pad()
                    inst = None
                    for hh in range(4):
                        sl = slice(hh * 128, (hh + 1) * 128)
                        inst = vec.scalar_tensor_tensor(
                            prescr2b[g % 2][:, sl], prescrb[g % 2][:, sl],
                            colg[:, G + 20 + hh:G + 21 + hh],
                            silu_g[:, g * 512 + hh * 128:g * 512 + (hh + 1) * 128],
                            ALU.add, ALU.mult)
                    vfin(inst, key)

                emit_vmask(0)
                emit_vmask(1)
                emit_vcomb(0)
                emit_vmask(2)
                emit_vcomb(1)
                emit_vpre(0)
                emit_vmask(3)
                emit_vcomb(2)
                emit_vpre(1)
                emit_vcomb(3)
                emit_vpre(2)
                emit_vpre(3)
                if c + 1 < NCH:
                    emit_vln(c + 1)
                    emit_vq(c + 1)
                emit_vres(c)
                emit_vfin(c)

        @block.scalar
        def _(rsc):
            ssc = [0]
            sc = _SelfSync(rsc, s_ss, ssc)

            def spad():
                rsc.wait_ge(s_ss, ssc[0])

            sc.memzero(padc[:, 1:2])
            sc.memzero(padc[:, 3:4])
            spad()

            def sfin(inst, key):
                rsc.wait_ge(s_ss, ssc[0])
                return fin(rsc.nop(), key)

            def emit_sevpre(cv, g):
                key = ("SEVPRE", cv, g)
                waits(sc, key)
                sfin(sc.copy(preT[:, g * 512:(g + 1) * 512],
                             ptall[:, (g % 2) * 512:(g % 2) * 512 + 512]), key)

            for c in range(NCH):
                CO = 24 + 12 * (c % 2)
                key = ("SXN", c)
                waits(sc, key)
                if c == 0:
                    sc.activation(padc[:, 1:2], padc[:, 3:4], AF.Sqrt)
                    spad()
                sc.activation(cols[:, CO + 11:CO + 12], cols[:, CO + 4:CO + 5], AF.Sqrt)
                spad()
                spad()
                sfin(sc.activation(xn16[:], centb[c % 2][:], AF.Copy,
                                   scale=cols[:, CO + 11:CO + 12]), key)
                key = ("SEVXNT", c)
                waits(sc, key)
                sc.copy(xnT[:, 0:512], ptall[:, 0:512])
                sfin(sc.copy(xnT[:, 512:1024], ptall[:, 512:1024]), key)
                key = ("SEVXQ", c)
                waits(sc, key)
                sc.copy(xqT[:, 0:512], ptall[:, 0:512])
                sfin(sc.copy(xqT[:, 512:1024], ptall[:, 512:1024]), key)
                for r_ in range(2):
                    key = ("SEVQ", c, r_)
                    waits(sc, key)
                    sfin(sc.copy(qT[0:64, r_ * 1024:(r_ + 1) * 1024], ptall[0:64, 0:1024]), key)
                for r_ in range(2):
                    key = ("SEVK", c, r_)
                    waits(sc, key)
                    sfin(sc.copy(kT[0:64, r_ * 1024:(r_ + 1) * 1024], ptall[0:64, 0:1024]), key)
                key = ("SSILU", c)
                waits(sc, key)
                sc.activation(padc[:, 1:2], padc[:, 3:4], AF.Silu)
                spad()
                for nb in range(3):
                    sc.activation(silu_g[:, nb * 512:(nb + 1) * 512],
                                  pbig[:, nb * 512:(nb + 1) * 512], AF.Silu)
                sfin(sc.activation(silu_g[:, 1536:2048], pbig[:, 1536:2048], AF.Silu), key)
                def emit_ssq(g):
                    key = ("SSQ", c, g)
                    waits(sc, key)
                    G = 56 * (g % 2)
                    junk = centb[(c + 1) % 2]
                    for hh in range(4):
                        sc.activation(junk[:, hh * 128:(hh + 1) * 128],
                                      ob_ap(c, g)[:, hh * 128:(hh + 1) * 128], AF.Copy,
                                      accum_out=colg[:, G + hh:G + hh + 1])
                    spad()
                    inst = None
                    for hh in range(4):
                        inst = sc.activation(junk[:, hh * 128:(hh + 1) * 128],
                                             ob_ap(c, g)[:, hh * 128:(hh + 1) * 128], AF.Square,
                                             accum_out=colg[:, G + 44 + hh:G + 45 + hh])
                    sfin(inst, key)

                def emit_sapply(g):
                    key = ("SAPPLY", c, g)
                    waits(sc, key)
                    G = 56 * (g % 2)
                    if c == 0 and g == 0:
                        sc.activation(padc[:, 1:2], padc[:, 3:4], AF.Sqrt)
                        spad()
                    sc.activation(colg[:, G + 8:G + 12], colg[:, G + 52:G + 56], AF.Sqrt)
                    spad()
                    for hh in range(4):
                        sc.activation(colg[:, G + 20 + hh:G + 21 + hh],
                                      colg[:, G + 40 + hh:G + 41 + hh],
                                      AF.Copy, scale=colg[:, G + 8 + hh:G + 9 + hh])
                    inst = None
                    for hh in range(4):
                        inst = sc.activation(prescrb[g % 2][:, hh * 128:(hh + 1) * 128],
                                             ob_ap(c, g)[:, hh * 128:(hh + 1) * 128], AF.Copy,
                                             scale=colg[:, G + 8 + hh:G + 9 + hh])
                    sfin(inst, key)

                emit_ssq(0)
                emit_ssq(1)
                emit_sapply(0)
                emit_ssq(2)
                emit_sapply(1)
                emit_ssq(3)
                emit_sevpre(c, 0)
                emit_sapply(2)
                emit_sevpre(c, 1)
                emit_sapply(3)
                emit_sevpre(c, 2)
                emit_sevpre(c, 3)

        @block.tensor
        def _(t):
            for kf in range(8):
                key = ("WQZT", kf)
                waits(t, key)
                inst = None
                for b in range(8):
                    inst = t.transpose(ptall[:, b * 128:(b + 1) * 128],
                                       xq16[:, b * 128:(b + 1) * 128], ident[:])
                fin(inst, key)
            def emit_txn(cv):
                key = ("TXN", cv)
                waits(t, key)
                inst = None
                for b in range(8):
                    inst = t.transpose(ptall[:, b * 128:(b + 1) * 128],
                                       xn16[:, b * 128:(b + 1) * 128], ident[:])
                fin(inst, key)

            def emit_txq(cv):
                key = ("TXQ", cv)
                waits(t, key)
                inst = None
                for b in range(8):
                    inst = t.transpose(ptall[:, b * 128:(b + 1) * 128],
                                       xq16[:, b * 128:(b + 1) * 128], ident[:])
                fin(inst, key)

            emit_txn(0)
            emit_txq(0)
            for c in range(NCH):
                for key_name, w_s, bias_off, pdst in [("TQ", wq, 0, pq), ("TK", wk, D, pbig)]:
                    key = (key_name, c)
                    waits(t, key)
                    inst = None
                    for n in range(2):
                        for k in range(8):
                            t.matmul(pdst[:, n * 512:(n + 1) * 512],
                                     xnT[:, k * 128:(k + 1) * 128],
                                     w_s[:, k * D + n * 512:k * D + (n + 1) * 512],
                                     start=(k == 0), stop=False)
                        inst = t.matmul(pdst[:, n * 512:(n + 1) * 512], ones_row[0:1, :],
                                        biasr[0:1, bias_off + n * 512:bias_off + (n + 1) * 512],
                                        start=False, stop=True)
                    fin(inst, key)
                if c >= 1:
                    emit_tstu(c - 1, 0)
                for src16, tname in [(q16, "TTRQ"), (k16, "TTRK")]:
                    for r_ in range(2):
                        key = (tname, c, r_)
                        waits(t, key)
                        inst = None
                        for hh in range(8):
                            h = r_ * 8 + hh
                            inst = t.transpose(ptall[0:64, hh * 128:(hh + 1) * 128],
                                               src16[:, h * DK:(h + 1) * DK], ident[:])
                        fin(inst, key)
                for k in range(8):
                    if c >= 1 and k < 7:
                        emit_tstu(c - 1, k + 1)
                    key = ("TV", c, k)
                    waits(t, key)
                    inst = None
                    for n in range(4):
                        inst = t.matmul(pbig[:, n * 512:(n + 1) * 512],
                                        xnT[:, k * 128:(k + 1) * 128],
                                        wgb[(c * 16 + k) % WGB_N][:, n * 512:(n + 1) * 512],
                                        start=(k == 0), stop=False)
                    if k == 7:
                        for n in range(4):
                            inst = t.matmul(
                                pbig[:, n * 512:(n + 1) * 512], ones_row[0:1, :],
                                biasr[0:1, 2 * D + n * 512:2 * D + (n + 1) * 512],
                                start=False, stop=True)
                    fin(inst, key)
                for k in range(8):
                    key = ("TG", c, k)
                    waits(t, key)
                    inst = None
                    for n in range(4):
                        inst = t.matmul(pbig[:, n * 512:(n + 1) * 512],
                                        xnT[:, k * 128:(k + 1) * 128],
                                        wgb[(c * 16 + 8 + k) % WGB_N][:, n * 512:(n + 1) * 512],
                                        start=(k == 0), stop=False)
                    if k == 7:
                        for n in range(4):
                            inst = t.matmul(
                                pbig[:, n * 512:(n + 1) * 512], ones_row[0:1, :],
                                biasr[0:1, 2 * D + DD + n * 512:2 * D + DD + (n + 1) * 512],
                                start=False, stop=True)
                    fin(inst, key)
                def emit_ttrp(cv, g):
                    key = ("TTRP", cv, g)
                    waits(t, key)
                    inst = None
                    for hh in range(4):
                        inst = t.transpose(
                            ptall[:, (g % 2) * 512 + hh * 128:(g % 2) * 512 + (hh + 1) * 128],
                            prescr2b[g % 2][:, hh * 128:(hh + 1) * 128], ident[:])
                    fin(inst, key)

                def emit_tstu(cv, pr):
                    key = ("TSTU", cv, pr)
                    waits(t, key)
                    inst = None
                    for hh in range(2):
                        h = pr * 2 + hh
                        inst = t.matmul(pq[0:64, hh * 129:(hh + 1) * 129],
                                        krgb[cv % 2][:, h * DK:(h + 1) * DK],
                                        vaug[:, h * 129:(h + 1) * 129],
                                        start=True, stop=True)
                    fin(inst, key)

                for g in range(NG):
                    key = ("TSC", c, g)
                    waits(t, key)
                    inst = None
                    for hh in range(4):
                        h = g * 4 + hh
                        inst = t.matmul(psc[:, hh * 128:(hh + 1) * 128],
                                        kT[0:64, h * 128:(h + 1) * 128],
                                        qT[0:64, h * 128:(h + 1) * 128],
                                        start=True, stop=True)
                    fin(inst, key)
                    key = ("TOUT", c, g)
                    waits(t, key)
                    obo = ob_off(c, g)
                    msrc = mskb[g % 2]
                    inst = None
                    for hh in range(4):
                        h = g * 4 + hh
                        t.matmul(pbig[:, obo + hh * 128:obo + (hh + 1) * 128],
                                 msrc[:, hh * 128:(hh + 1) * 128],
                                 vaug[:, h * 129:h * 129 + 128], start=True, stop=False)
                        t.matmul(pbig[:, SIG0 + h:SIG0 + h + 1],
                                 msrc[:, hh * 128:(hh + 1) * 128],
                                 ones_col[:, 0:1], start=True, stop=False)
                        qslice = qT[0:64, h * 128:(h + 1) * 128]
                        t.matmul(pbig[:, obo + hh * 128:obo + (hh + 1) * 128],
                                 qslice, st_hi[0:64, h * 129:h * 129 + 128],
                                 start=False, stop=False)
                        t.matmul(pbig[:, obo + hh * 128:obo + (hh + 1) * 128],
                                 qslice, st_lo[0:64, h * 129:h * 129 + 128],
                                 start=False, stop=True)
                        t.matmul(pbig[:, SIG0 + h:SIG0 + h + 1],
                                 qslice, st_hi[0:64, h * 129 + 128:h * 129 + 129],
                                 start=False, stop=False)
                        inst = t.matmul(pbig[:, SIG0 + h:SIG0 + h + 1],
                                        qslice, st_lo[0:64, h * 129 + 128:h * 129 + 129],
                                        start=False, stop=True)
                    fin(inst, key)
                emit_ttrp(c, 0)
                emit_ttrp(c, 1)
                emit_ttrp(c, 2)
                emit_ttrp(c, 3)
                key = ("TFFN", c)
                waits(t, key)
                inst = None
                for n in range(2):
                    for k in range(8):
                        inst = t.matmul(pbig[:, n * 512:(n + 1) * 512],
                                        xqT[:, k * 128:(k + 1) * 128],
                                        wqT[:, k * D + n * 512:k * D + (n + 1) * 512],
                                        start=(k == 0), stop=(k == 7))
                fin(inst, key)
                for k in range(16):
                    key = ("TWO", c, k)
                    waits(t, key)
                    inst = None
                    for n in range(2):
                        inst = t.matmul(pq[:, n * 512:(n + 1) * 512],
                                        preT[:, k * 128:(k + 1) * 128],
                                        wo_r[:, k * D + n * 512:k * D + (n + 1) * 512],
                                        start=(k == 0), stop=False)
                    if k == 15:
                        for n in range(2):
                            t.matmul(pq[:, n * 512:(n + 1) * 512], ones_row[0:1, :],
                                     biasr[0:1, 2 * D + 2 * DD + n * 512:
                                           2 * D + 2 * DD + (n + 1) * 512],
                                     start=False, stop=False)
                            inst = t.matmul(pq[:, n * 512:(n + 1) * 512], ones_row[0:1, :],
                                            biasr[0:1, 3 * D + 2 * DD + n * 512:
                                                  3 * D + 2 * DD + (n + 1) * 512],
                                            start=False, stop=True)
                    fin(inst, key)
                if c + 1 < NCH:
                    emit_txn(c + 1)
                    emit_txq(c + 1)

    return nc


def kernel(L0_attn_x, L0_attn_Wq, L0_attn_bq, L0_attn_Wk, L0_attn_bk,
           L0_attn_Wv, L0_attn_bv, L0_attn_Wg, L0_attn_bg, L0_attn_Wo,
           L0_attn_bo, L0_ffn_weight, L0_ffn_bias):
    hc = _HC
    perm = np.concatenate([np.arange(0, DK, 2), np.arange(1, DK, 2)])
    full_perm = np.concatenate([h * DK + perm for h in range(H)])
    wq_p = np.ascontiguousarray(np.asarray(L0_attn_Wq, np.float32)[:, full_perm])
    wk_p = np.ascontiguousarray(np.asarray(L0_attn_Wk, np.float32)[:, full_perm])
    bq_p = np.ascontiguousarray(np.asarray(L0_attn_bq, np.float32)[full_perm].reshape(1, D))
    bk_p = np.ascontiguousarray(np.asarray(L0_attn_bk, np.float32)[full_perm].reshape(1, D))

    common = dict(
        wq=wq_p, wk=wk_p,
        wv=np.ascontiguousarray(np.asarray(L0_attn_Wv, np.float32)),
        wg=np.ascontiguousarray(np.asarray(L0_attn_Wg, np.float32)),
        wo=np.ascontiguousarray(np.asarray(L0_attn_Wo, np.float32)),
        wf=np.ascontiguousarray(np.asarray(L0_ffn_weight, np.float32)),
        bq=bq_p, bk=bk_p,
        bv=np.asarray(L0_attn_bv, np.float32).reshape(1, DD),
        bg=np.asarray(L0_attn_bg, np.float32).reshape(1, DD),
        bo=np.asarray(L0_attn_bo, np.float32).reshape(1, D),
        bf=np.asarray(L0_ffn_bias, np.float32).reshape(1, D),
        cosC=hc["cosC"], sinC=hc["sinC"], qg=hc["qg"], kg=hc["kg"],
        kg2=hc["kg2"], causal=hc["causal"], r_t=hc["r"], rinv_t=hc["rinv"],
        ident=hc["ident"], identf=hc["identf"], ones_row=hc["ones_row"],
        ones_col=hc["ones_col"],
    )
    xs = np.asarray(L0_attn_x, np.float32)
    in_maps = [dict(common, x=np.ascontiguousarray(xs[b])) for b in range(8)]
    nc = build_nc()
    res = run_bass_kernel_spmd(nc, in_maps, core_ids=list(range(8)))
    return np.stack([np.asarray(r["y"], np.float32) for r in res.results], axis=0)


# version 29 (final)
# speedup vs baseline: 1.0061x; 1.0061x over previous
import numpy as np
from contextlib import ExitStack

import concourse.bass as bass
import concourse.mybir as mybir
from concourse.bass_utils import run_bass_kernel_spmd

F32 = mybir.dt.float32
BF16 = mybir.dt.bfloat16
I32 = mybir.dt.int32
AF = mybir.ActivationFunctionType
ALU = mybir.AluOpType
AX = mybir.AxisListType

S, D, DD, H, DK, DV, C = 4096, 1024, 2048, 16, 64, 128, 128
NCH = S // C
NG = 4
EPS_LN = 1e-5
EPS_Q = 1e-8
QMAX = 127.0
WGB_N = 5
SERIAL = False
SERIAL2 = False
SERIAL_VS = False
SERIAL_TV = False
SERIAL_TS = False
SERIAL_ATT = False
DBG = False
DETGATE = True


def _host_consts():
    gam = 1.0 - 2.0 ** (-5.0 - np.arange(H, dtype=np.float64))
    idx = np.arange(S, dtype=np.float64)
    ang = 1.0 / 10000.0 ** np.linspace(0.0, 1.0, DK // 2)
    th = idx[:, None] * ang[None, :]
    cos32 = np.cos(th).astype(np.float32)
    sin32 = np.sin(th).astype(np.float32)
    cosC = np.concatenate([cos32, cos32], 1)
    sinC = np.concatenate([-sin32, sin32], 1)
    j = np.arange(C, dtype=np.float64)
    qg = (gam[None, :] ** j[:, None]).astype(np.float32)
    kg = (0.125 * gam[None, :] ** (C - j)[:, None]).astype(np.float32)
    kg2 = (0.125 * gam[None, :] ** (-j)[:, None]).astype(np.float32)
    causal = (j[None, :] >= j[:, None]).astype(np.float32)
    rowsum = (1.0 - gam[None, :] ** (idx[:, None] + 1.0)) / (1.0 - gam[None, :])
    r = (rowsum ** -0.5).reshape(NCH, C, H).transpose(1, 0, 2).reshape(C, NCH * H).astype(np.float32)
    rinv = (rowsum ** 0.5).reshape(NCH, C, H).transpose(1, 0, 2).reshape(C, NCH * H).astype(np.float32)
    import ml_dtypes
    ident = np.eye(128, dtype=ml_dtypes.bfloat16)
    identf = np.eye(128, dtype=np.float32)
    ones_row = np.ones((1, 128), dtype=ml_dtypes.bfloat16)
    ones_col = np.ones((128, 1), dtype=ml_dtypes.bfloat16)
    return dict(cosC=cosC, sinC=sinC, qg=qg, kg=kg, kg2=kg2, causal=causal,
                r=r, rinv=rinv, gamC=(gam ** C).astype(np.float64),
                ident=ident, identf=identf, ones_row=ones_row, ones_col=ones_col)


_HC = _host_consts()


class _SelfSync:

    def __init__(self, eng, sem, ctr):
        self._e = eng
        self._s = sem
        self._c = ctr

    def __getattr__(self, name):
        f = getattr(self._e, name)
        if name == "wait_ge":
            return f

        def g(*a, **k):
            inst = f(*a, **k)
            inst.then_inc(self._s, 1)
            self._c[0] += 1
            return inst
        return g


class Plan:
    def __init__(self):
        self.count = {}
        self.phase = {}
        self.keys = []

    def add(self, key, sem_name, inc):
        assert key not in self.phase, key
        self.count[sem_name] = self.count.get(sem_name, 0) + inc
        self.phase[key] = (sem_name, self.count[sem_name])
        self.keys.append(key)


def build_nc():
    nc = bass.Bass("TRN2", target_bir_lowering=False, detect_race_conditions=False)

    x_h = nc.dram_tensor("x", [S, D], F32, kind="ExternalInput").ap()
    wq_h = nc.dram_tensor("wq", [D, D], F32, kind="ExternalInput").ap()
    wk_h = nc.dram_tensor("wk", [D, D], F32, kind="ExternalInput").ap()
    wv_h = nc.dram_tensor("wv", [D, DD], F32, kind="ExternalInput").ap()
    wg_h = nc.dram_tensor("wg", [D, DD], F32, kind="ExternalInput").ap()
    wo_h = nc.dram_tensor("wo", [DD, D], F32, kind="ExternalInput").ap()
    wf_h = nc.dram_tensor("wf", [D, D], F32, kind="ExternalInput").ap()
    bq_h = nc.dram_tensor("bq", [1, D], F32, kind="ExternalInput").ap()
    bk_h = nc.dram_tensor("bk", [1, D], F32, kind="ExternalInput").ap()
    bv_h = nc.dram_tensor("bv", [1, DD], F32, kind="ExternalInput").ap()
    bg_h = nc.dram_tensor("bg", [1, DD], F32, kind="ExternalInput").ap()
    bo_h = nc.dram_tensor("bo", [1, D], F32, kind="ExternalInput").ap()
    bf_h = nc.dram_tensor("bf", [1, D], F32, kind="ExternalInput").ap()
    cos_h = nc.dram_tensor("cosC", [S, DK], F32, kind="ExternalInput").ap()
    sin_h = nc.dram_tensor("sinC", [S, DK], F32, kind="ExternalInput").ap()
    qg_h = nc.dram_tensor("qg", [C, H], F32, kind="ExternalInput").ap()
    kg_h = nc.dram_tensor("kg", [C, H], F32, kind="ExternalInput").ap()
    kg2_h = nc.dram_tensor("kg2", [C, H], F32, kind="ExternalInput").ap()
    causal_h = nc.dram_tensor("causal", [C, C], F32, kind="ExternalInput").ap()
    r_h = nc.dram_tensor("r_t", [C, NCH * H], F32, kind="ExternalInput").ap()
    rinv_h = nc.dram_tensor("rinv_t", [C, NCH * H], F32, kind="ExternalInput").ap()
    ident_h = nc.dram_tensor("ident", [128, 128], BF16, kind="ExternalInput").ap()
    onesr_h = nc.dram_tensor("ones_row", [1, 128], BF16, kind="ExternalInput").ap()
    onesc_h = nc.dram_tensor("ones_col", [128, 1], BF16, kind="ExternalInput").ap()
    y_h = nc.dram_tensor("y", [S, D], F32, kind="ExternalOutput").ap()
    dbg_specs = []
    wg_sc = nc.dram_tensor("wg_scr", [D, DD], BF16, kind="Internal").ap()
    wv_sc = nc.dram_tensor("wv_scr", [D, DD], BF16, kind="Internal").ap()
    gamC = _HC["gamC"]

    with ExitStack() as st:
        def sb(name, shape, dt):
            return st.enter_context(nc.sbuf_tensor(name, shape, dt))

        wq = sb("wq_s", [128, 8 * D], BF16)
        wk = sb("wk_s", [128, 8 * D], BF16)
        wqT = sb("wqT_s", [128, 8 * D], BF16)
        wgb = [sb(f"wgb{i}", [128, DD], BF16) for i in range(WGB_N)]
        wo_r = sb("wo_r", [128, 16 * D], BF16)
        cosb = [sb(f"cosb{i}", [128, DK], F32) for i in range(2)]
        sinb = [sb(f"sinb{i}", [128, DK], F32) for i in range(2)]
        qg = sb("qg_s", [128, H], F32)
        kg = sb("kg_s", [128, H], F32)
        kg2 = sb("kg2_s", [128, H], F32)
        causal = sb("causal_s", [128, C], F32)
        r_t = sb("r_s", [128, NCH * H], F32)
        rinv_t = sb("rinv_s", [128, NCH * H], F32)
        ident = sb("ident_s", [128, 128], BF16)
        ones_row = sb("onesr_s", [1, 128], BF16)
        ones_col = sb("onesc_s", [128, 1], BF16)
        NBIAS = 2 * D + 2 * DD + 2 * D
        biasr = sb("biasr", [1, NBIAS], BF16)
        xb = [sb(f"xb{i}", [128, D], F32) for i in range(2)]
        centb = [sb(f"cent{i}", [128, D], F32) for i in range(2)]
        p1b = [sb(f"p1_{i}", [128, D], F32) for i in range(2)]
        cent = centb[0]
        p1 = p1b[0]
        p3 = sb("p3", [128, D // 2], I32)
        xn16 = sb("xn16", [128, D], BF16)
        xnT = sb("xnT", [128, 8 * 128], BF16)
        xqT = sb("xqT", [128, 8 * 128], BF16)
        q16 = sb("q16", [128, D], BF16)
        xq16 = q16
        k16 = xn16
        krgb = [sb(f"krg{i}", [128, D], BF16) for i in range(2)]
        qT = sb("qT", [64, H * 128], BF16)
        kT = sb("kT", [64, H * 128], BF16)
        vaug = sb("vaug", [128, H * 129], BF16)
        silu_g = sb("silu_g", [128, DD], BF16)
        stbuf = [sb(f"state{i}", [64, H * 129], F32) for i in range(2)]
        st_hi = sb("st_hi", [64, H * 129], BF16)
        st_lo = sb("st_lo", [64, H * 129], BF16)
        mskb = [sb(f"msk{i}", [128, 512], BF16) for i in range(2)]
        preT = silu_g
        prescrb = [sb(f"prescr{i}", [128, 512], BF16) for i in range(2)]
        prescr2b = [sb(f"prescr2{i}", [128, 512], BF16) for i in range(2)]
        cols = sb("cols", [128, 48], F32)
        padc = sb("padc", [128, 4], F32)
        gdump = sb("gdump", [128, 128], F32) if DBG else None
        colg = sb("colg", [128, 112], F32)

        if DBG:
            for nm, buf in [("d_xn16", xn16), ("d_xq16", xq16), ("d_q16", q16),
                            ("d_k16", k16), ("d_krg", krg), ("d_vaug", vaug),
                            ("d_qT", qT), ("d_kT", kT), ("d_msk", msk),
                            ("d_silu", silu_g), ("d_sthi", st_hi), ("d_stlo", st_lo),
                            ("d_preT", preT), ("d_prescr2", prescr2),
                            ("d_colg", colg), ("d_cols", cols), ("d_xnT", xnT),
                            ("d_xqT", xqT), ("d_wqT", wqT), ("d_st0", stbuf[0]),
                            ("d_st1", stbuf[1]), ("d_gdump", gdump)]:
                hshape = [buf.shape[0] if hasattr(buf, 'shape') else 128, 0]
                dbg_specs.append((nm, buf))
        dbg_aps = []
        for nm, buf in dbg_specs:
            bb = buf[:]
            dbg_aps.append((nc.dram_tensor(nm, [bb.shape[0], bb.shape[1]],
                                           bb.dtype, kind="ExternalOutput").ap(), buf))

        pq = st.enter_context(nc.psum_tensor("pq", [128, D], F32))
        pbig = st.enter_context(nc.psum_tensor("pbig", [128, DD], F32))
        ptall = st.enter_context(nc.psum_tensor("ptall", [128, 1024], BF16))
        psc = st.enter_context(nc.psum_tensor("psc", [128, 512], F32))

        s_x = st.enter_context(nc.semaphore("s_x"))
        s_w = st.enter_context(nc.semaphore("s_w"))
        s_q = st.enter_context(nc.semaphore("s_q"))
        s_T = st.enter_context(nc.semaphore("s_T"))
        s_V = st.enter_context(nc.semaphore("s_V"))
        s_S = st.enter_context(nc.semaphore("s_S"))
        s_y = st.enter_context(nc.semaphore("s_y"))
        s_vv = st.enter_context(nc.semaphore("s_vv"))
        s_ss = st.enter_context(nc.semaphore("s_ss"))
        s_wg = [st.enter_context(nc.semaphore(f"s_wg{i}")) for i in range(WGB_N)]
        SEMS = dict(s_x=s_x, s_w=s_w, s_T=s_T, s_V=s_V, s_S=s_S, s_y=s_y)
        for i in range(WGB_N):
            SEMS[f"s_wg{i}"] = s_wg[i]

        SIG0 = 1024 + 300
        DST0 = 1024

        plan = Plan()
        plan.add(("LDC",), "s_x", 16 * 12)
        plan.add(("LDW",), "s_w", 16 * 22)
        plan.add(("LDSC",), "s_w", 16 * 48)
        for kf in range(8):
            plan.add(("WFL", kf), "s_x", 16)
            plan.add(("WQZV", kf), "s_V", 1)
            plan.add(("WQZT", kf), "s_T", 1)
            plan.add(("WQZS", kf), "s_V", 1)
        plan.add(("VINIT",), "s_V", 1)
        plan.add(("VLN", 0), "s_V", 1)
        plan.add(("VQ", 0), "s_V", 1)
        plan.add(("TXN", 0), "s_T", 1)
        plan.add(("TXQ", 0), "s_T", 1)
        for c in range(NCH):
            plan.add(("X", c), "s_x", 16)
            plan.add(("TRG", c), "s_x", 32)
            for k in range(8):
                plan.add(("WV", c, k), f"s_wg{(c * 16 + k) % WGB_N}", 16)
            for k in range(8):
                plan.add(("WG", c, k), f"s_wg{(c * 16 + 8 + k) % WGB_N}", 16)
            plan.add(("SXN", c), "s_S", 1)
            plan.add(("SEVXNT", c), "s_S", 1)
            plan.add(("SEVXQ", c), "s_S", 1)
            plan.add(("TQ", c), "s_T", 1)
            plan.add(("VROPEQ", c), "s_V", 1)
            plan.add(("TK", c), "s_T", 1)
            plan.add(("VROPEK", c), "s_V", 1)
            if c >= 1:
                plan.add(("TSTU", c - 1, 0), "s_T", 1)
                for pr in range(8):
                    plan.add(("VSTU", c - 1, pr), "s_V", 1)
                plan.add(("VSTH", c - 1), "s_V", 1)
            for r_ in range(2):
                plan.add(("TTRQ", c, r_), "s_T", 1)
                plan.add(("SEVQ", c, r_), "s_S", 1)
            for r_ in range(2):
                plan.add(("TTRK", c, r_), "s_T", 1)
                plan.add(("SEVK", c, r_), "s_S", 1)
            for k in range(8):
                if c >= 1 and k < 7:
                    plan.add(("TSTU", c - 1, k + 1), "s_T", 1)
                plan.add(("TV", c, k), "s_T", 1)
            plan.add(("VEVV", c), "s_V", 1)
            for k in range(8):
                plan.add(("TG", c, k), "s_T", 1)
            plan.add(("SSILU", c), "s_S", 1)
            if DBG:
                plan.add(("VGD", c), "s_V", 1)
            for g in range(NG):
                plan.add(("TSC", c, g), "s_T", 1)
                plan.add(("TOUT", c, g), "s_T", 1)
            plan.add(("TTRP", c, 0), "s_T", 1)
            plan.add(("TTRP", c, 1), "s_T", 1)
            plan.add(("VMASK", c, 0), "s_V", 1)
            plan.add(("VMASK", c, 1), "s_V", 1)
            plan.add(("VCOMB", c, 0), "s_V", 1)
            plan.add(("VMASK", c, 2), "s_V", 1)
            plan.add(("VCOMB", c, 1), "s_V", 1)
            plan.add(("VPRE", c, 0), "s_V", 1)
            plan.add(("VMASK", c, 3), "s_V", 1)
            plan.add(("VCOMB", c, 2), "s_V", 1)
            plan.add(("VPRE", c, 1), "s_V", 1)
            plan.add(("VCOMB", c, 3), "s_V", 1)
            plan.add(("VPRE", c, 2), "s_V", 1)
            plan.add(("VPRE", c, 3), "s_V", 1)
            plan.add(("SSQ", c, 0), "s_S", 1)
            plan.add(("SSQ", c, 1), "s_S", 1)
            plan.add(("SAPPLY", c, 0), "s_S", 1)
            plan.add(("SSQ", c, 2), "s_S", 1)
            plan.add(("SAPPLY", c, 1), "s_S", 1)
            plan.add(("SSQ", c, 3), "s_S", 1)
            plan.add(("SEVPRE", c, 0), "s_S", 1)
            plan.add(("SAPPLY", c, 2), "s_S", 1)
            plan.add(("SEVPRE", c, 1), "s_S", 1)
            plan.add(("SAPPLY", c, 3), "s_S", 1)
            plan.add(("SEVPRE", c, 2), "s_S", 1)
            plan.add(("SEVPRE", c, 3), "s_S", 1)
            plan.add(("TTRP", c, 2), "s_T", 1)
            plan.add(("TTRP", c, 3), "s_T", 1)
            if c + 1 < NCH:
                plan.add(("VLN", c + 1), "s_V", 1)
                plan.add(("VQ", c + 1), "s_V", 1)
            plan.add(("TFFN", c), "s_T", 1)
            for k in range(16):
                plan.add(("TWO", c, k), "s_T", 1)
            if c + 1 < NCH:
                plan.add(("TXN", c + 1), "s_T", 1)
                plan.add(("TXQ", c + 1), "s_T", 1)
            plan.add(("VRES", c), "s_V", 1)
            plan.add(("VFIN", c), "s_V", 1)
            if c >= 1:
                plan.add(("Y", c - 1), "s_y", 16)
        plan.add(("Y", NCH - 1), "s_y", 16)
        if DBG:
            plan.add(("DBGD",), "s_y", 16 * len(dbg_specs))

        DEPS = {}

        def dd(key, *deps):
            DEPS[key] = [d for d in deps if d is not None]

        def pv(name, c, *rest):
            return (name, c, *rest) if c >= 0 else None

        dd(("LDC",))
        dd(("LDW",))
        dd(("LDSC",))
        for kf in range(8):
            dd(("WFL", kf), ("WQZV", kf - 1) if kf else None)
            dd(("WQZV", kf), ("WFL", kf), ("WQZS", kf - 1) if kf else None)
            dd(("WQZT", kf), ("WQZV", kf), ("LDC",),
               ("WQZS", kf - 1) if kf else None)
            dd(("WQZS", kf), ("WQZT", kf))
        dd(("VINIT",), ("WQZS", 7))
        if DBG:
            dd(("DBGD",), ("VFIN", NCH - 1), ("Y", NCH - 1))
        for c in range(NCH):
            dd(("X", c), pv("Y", c - 2))
            dd(("TRG", c), pv("VROPEK", c - 2))
            for k in range(8):
                dd(("WV", c, k),
                   ("TV", c, k - WGB_N) if k >= WGB_N else
                   (pv("TG", c - 1, k + 8 - WGB_N) if c >= 1 else ("LDSC",)),
                   ("TV", c, k - 1) if (SERIAL2 and k >= 1) else None)
            for k in range(8):
                dd(("WG", c, k),
                   ("TG", c, k - WGB_N) if k >= WGB_N else ("TV", c, k + 8 - WGB_N),
                   ("TG", c, k - 1) if (SERIAL2 and k >= 1) else None)
            dd(("VLN", c), ("X", c), ("LDC",), pv("SSQ", c - 1, 3))
            dd(("SXN", c), ("VLN", c), pv("TTRK", c - 1, 1))
            dd(("VQ", c), ("VLN", c), pv("TXQ", c - 1), ("VINIT",))
            dd(("TXN", c), ("SXN", c), pv("SEVPRE", c - 1, 3), ("WQZS", 7))
            dd(("SEVXNT", c), ("TXN", c), pv("TG", c - 1, 7))
            dd(("TXQ", c), ("VQ", c), ("SEVXNT", c))
            dd(("SEVXQ", c), ("TXQ", c), pv("TFFN", c - 1))
            dd(("TQ", c), ("SEVXNT", c), ("LDW",), pv("VRES", c - 1), ("WQZS", 7))
            dd(("VROPEQ", c), ("TQ", c), ("TRG", c), ("VQ", c),
               pv("TTRQ", c - 1, 1))
            dd(("TK", c), ("SEVXNT", c), ("LDW",), pv("VFIN", c - 1), ("WQZS", 7))
            dd(("VROPEK", c), ("TK", c), pv("TTRK", c - 1, 1))
            dd(("TTRQ", c, 0), ("VROPEQ", c), ("SEVXQ", c), pv("TOUT", c - 1, 3))
            dd(("SEVQ", c, 0), ("TTRQ", c, 0))
            dd(("TTRQ", c, 1), ("SEVQ", c, 0))
            dd(("SEVQ", c, 1), ("TTRQ", c, 1))
            dd(("TTRK", c, 0), ("VROPEK", c), ("SEVQ", c, 1))
            dd(("SEVK", c, 0), ("TTRK", c, 0))
            dd(("TTRK", c, 1), ("SEVK", c, 0))
            dd(("SEVK", c, 1), ("TTRK", c, 1))
            for k in range(8):
                dd(("TV", c, k), ("WV", c, k), ("LDW",),
                   ("VROPEK", c) if k == 0 else ("TV", c, k - 1),
                   pv("VFIN", c - 1) if k == 0 else None,
                   pv("VCOMB", c - 1, 3) if k == 0 else None)
            dd(("VEVV", c), ("TV", c, 7), pv("TSTU", c - 1, 7), ("VINIT",))
            for k in range(8):
                dd(("TG", c, k),
                   ("VEVV", c) if k == 0 else ("TG", c, k - 1),
                   ("WG", c, k), ("LDW",),
                   pv("VPRE", c - 1, 3) if k == 0 else None)
            dd(("SSILU", c), ("TG", c, 7))
            if DBG:
                dd(("VGD", c), ("SSILU", c))
            for g in range(NG):
                dd(("TSC", c, g), ("SEVQ", c, 1), ("SEVK", c, 1),
                   ("VMASK", c, g - 1) if g >= 1 else
                   (pv("VMASK", c - 1, 3) if c >= 1 else ("WQZS", 7)))
                dd(("VMASK", c, g), ("TSC", c, g), ("LDC",),
                   ("TOUT", c, g - 2) if g >= 2 else pv("TOUT", c - 1, g + 2))
                dd(("TOUT", c, g), ("VMASK", c, g), ("VEVV", c), ("SSILU", c),
                   ("VGD", c) if DBG else None,
                   pv("VSTH", c - 1),
                   ("SSQ", c, g - 3) if g >= 3 else pv("SSQ", c - 1, g + 1),
                   ("SAPPLY", c, g - 3) if g >= 3 else pv("SAPPLY", c - 1, g + 1))
                dd(("SSQ", c, g), ("TOUT", c, g),
                   ("VPRE", c, g - 2) if g >= 2 else pv("VPRE", c - 1, g + 2))
                dd(("VCOMB", c, g), ("TOUT", c, g), ("SSQ", c, g))
                dd(("SAPPLY", c, g), ("VCOMB", c, g), ("SSILU", c),
                   ("VPRE", c, g - 2) if g >= 2 else pv("VPRE", c - 1, g + 2))
                dd(("VPRE", c, g), ("SAPPLY", c, g),
                   ("TTRP", c, g - 2) if g >= 2 else pv("TTRP", c - 1, g + 2))
                dd(("TTRP", c, g), ("VPRE", c, g), ("SEVK", c, 1),
                   ("SEVPRE", c, g - 1) if g >= 1 else None)
                dd(("SEVPRE", c, g), ("TTRP", c, g), pv("TWO", c - 1, 15))
            for pr in range(8):
                dd(("TSTU", c, pr), ("VROPEK", c), ("VEVV", c),
                   ("VROPEQ", c + 1) if c + 1 < NCH else None,
                   ("VSTU", c, pr - 1) if pr >= 1 else None)
                dd(("VSTU", c, pr), ("TSTU", c, pr),
                   pv("VSTH", c - 1))
            dd(("VSTH", c), ("VSTU", c, 7), ("TOUT", c, 3))
            dd(("TFFN", c), ("SEVXQ", c), ("SAPPLY", c, 3), ("SAPPLY", c, 2), ("WQZS", 7))
            for k in range(16):
                dd(("TWO", c, k), ("SEVPRE", c, k // 4), ("LDSC",), ("LDW",),
                   ("VROPEQ", c) if k == 0 else None,
                   pv("VSTU", c - 1, 7) if k == 0 else None)
            dd(("VRES", c), ("TWO", c, 15))
            dd(("VFIN", c), ("VRES", c), ("TFFN", c), ("SXN", c))
            dd(("Y", c), ("VFIN", c))

        if SERIAL:
            comp = [k for k in plan.keys
                    if plan.phase[k][0] in ("s_T", "s_V", "s_S")]
            for i in range(1, len(comp)):
                DEPS.setdefault(comp[i], []).append(comp[i - 1])
        elif SERIAL_VS or SERIAL_TV or SERIAL_TS:
            sel = ("s_V", "s_S") if SERIAL_VS else (("s_T", "s_V") if SERIAL_TV else ("s_T", "s_S"))
            comp = [k for k in plan.keys if plan.phase[k][0] in sel]
            for i in range(1, len(comp)):
                DEPS.setdefault(comp[i], []).append(comp[i - 1])
        elif SERIAL_ATT:
            names = ("VLN", "SXN", "VQ", "TXN", "SEVXNT", "TXQ", "SEVXQ",
                     "TQ", "VROPEQ", "TK", "VROPEK", "TTRQ", "SEVQ", "TTRK",
                     "SEVK", "TV", "VEVV", "TG", "SSILU", "TFFN", "TWO", "VFIN")
            comp = [k for k in plan.keys if k[0] in names]
            for i in range(1, len(comp)):
                DEPS.setdefault(comp[i], []).append(comp[i - 1])

        EFOR = {"s_V": "V", "s_S": "S", "s_T": "T", "s_x": "SY", "s_y": "SY",
                "s_w": "G"}
        for i in range(WGB_N):
            EFOR[f"s_wg{i}"] = "G"
        edges = {k: list(ds) for k, ds in DEPS.items()}
        prev_by_eng = {}
        for k in plan.keys:
            eng = EFOR[plan.phase[k][0]]
            if eng in prev_by_eng:
                edges.setdefault(k, []).append(prev_by_eng[eng])
            prev_by_eng[eng] = k
        color = {}

        def _cyc(node, path):
            color[node] = 1
            path.append(node)
            for dep_n in edges.get(node, []):
                st_ = color.get(dep_n, 0)
                if st_ == 1:
                    i0 = path.index(dep_n)
                    raise AssertionError(f"dependency cycle: {path[i0:] + [dep_n]}")
                if st_ == 0:
                    _cyc(dep_n, path)
            path.pop()
            color[node] = 2
        import sys as _sys
        _sys.setrecursionlimit(100000)
        for k in plan.keys:
            if color.get(k, 0) == 0:
                _cyc(k, [])

        def waits(eng, key):
            my_sem = plan.phase[key][0]
            best = {}
            for dep_key in DEPS.get(key, []):
                sname, cnt = plan.phase[dep_key]
                if sname == my_sem:
                    continue
                if best.get(sname, -1) < cnt:
                    best[sname] = cnt
            for sname, cnt in best.items():
                eng.wait_ge(SEMS[sname], cnt)

        def fin(inst, key):
            sname, _ = plan.phase[key]
            inc = 16 if sname.startswith(("s_x", "s_w", "s_y")) else 1
            inst.then_inc(SEMS[sname], inc)
            return inst

        def mkfin(padfn):
            def f(inst, key):
                padfn()
                last = padfn()
                sname, _ = plan.phase[key]
                inc = 16 if sname.startswith(("s_x", "s_w", "s_y")) else 1
                last.then_inc(SEMS[sname], inc)
                return last
            return f

        build_nc.plan = plan
        block = st.enter_context(nc.Block())

        @block.sync
        def _(sync):
            sxc = [0]
            syc = [0]

            def xgate():
                if DETGATE:
                    sync.wait_ge(s_x, sxc[0])
                sxc[0] += 16

            def ygate():
                if DETGATE:
                    sync.wait_ge(s_y, syc[0])
                syc[0] += 16

            for dst, src in [(qg[:], qg_h), (kg[:], kg_h), (kg2[:], kg2_h),
                             (causal[:], causal_h), (r_t[:], r_h),
                             (rinv_t[:], rinv_h), (ident[:], ident_h),
                             (ones_row[:], onesr_h),
                             (ones_col[:], onesc_h), (cosb[1][:], cos_h[0:C, :]),
                             (sinb[1][:], sin_h[0:C, :]),
                             (cosb[0][:], cos_h[0:C, :])]:
                xgate()
                sync.dma_start(dst, src).then_inc(s_x, 16)
            for kf in range(8):
                key = ("WFL", kf)
                waits(sync, key)
                xgate()
                fin(sync.dma_start(p1[:], wf_h[kf * 128:(kf + 1) * 128, :]), key)
            def emit_y(cv):
                key = ("Y", cv)
                waits(sync, key)
                ygate()
                fin(sync.dma_start(y_h[cv * C:(cv + 1) * C, :], xb[cv % 2][:]), key)

            for c in range(NCH):
                key = ("X", c)
                waits(sync, key)
                xgate()
                fin(sync.dma_start(xb[c % 2][:], x_h[c * C:(c + 1) * C, :]), key)
                key = ("TRG", c)
                waits(sync, key)
                xgate()
                sync.dma_start(cosb[c % 2][:], cos_h[c * C:(c + 1) * C, :]).then_inc(s_x, 16)
                xgate()
                sync.dma_start(sinb[c % 2][:], sin_h[c * C:(c + 1) * C, :]).then_inc(s_x, 16)
                if c >= 1:
                    emit_y(c - 1)
            emit_y(NCH - 1)
            if DBG:
                key = ("DBGD",)
                waits(sync, key)
                sync.wait_ge(s_T, plan.count["s_T"])
                sync.wait_ge(s_V, plan.count["s_V"])
                sync.wait_ge(s_S, plan.count["s_S"])
                for ap_, buf in dbg_aps:
                    sync.dma_start(ap_, buf[:]).then_inc(s_y, 16)
            sync.wait_ge(s_y, plan.count["s_y"])

        @block.gpsimd
        def _(g):
            gwc = [0]

            def wgate():
                if DETGATE:
                    g.wait_ge(s_w, gwc[0])
                gwc[0] += 16

            key = ("LDW",)
            for k in range(8):
                wgate()
                g.dma_start(wq[:, k * D:(k + 1) * D], wq_h[k * 128:(k + 1) * 128, :]).then_inc(s_w, 16)
            for k in range(8):
                wgate()
                g.dma_start(wk[:, k * D:(k + 1) * D], wk_h[k * 128:(k + 1) * 128, :]).then_inc(s_w, 16)
            for bslc, bh in [(biasr[0:1, 0:D], bq_h), (biasr[0:1, D:2 * D], bk_h),
                             (biasr[0:1, 2 * D:2 * D + DD], bv_h),
                             (biasr[0:1, 2 * D + DD:2 * D + 2 * DD], bg_h),
                             (biasr[0:1, 2 * D + 2 * DD:3 * D + 2 * DD], bo_h),
                             (biasr[0:1, 3 * D + 2 * DD:4 * D + 2 * DD], bf_h)]:
                wgate()
                g.dma_start(bslc, bh).then_inc(s_w, 16)
            key = ("LDSC",)
            base = plan.phase[("LDW",)][1]
            n = 0
            for k in range(8):
                wgate()
                g.dma_start(wgb[k % 2][:], wv_h[k * 128:(k + 1) * 128, :]).then_inc(s_w, 16)
                n += 1
                g.wait_ge(s_w, base + n * 16)
                wgate()
                g.dma_start(wv_sc[k * 128:(k + 1) * 128, :], wgb[k % 2][:]).then_inc(s_w, 16)
                n += 1
            for k in range(8):
                wgate()
                g.dma_start(wgb[k % 2][:], wg_h[k * 128:(k + 1) * 128, :]).then_inc(s_w, 16)
                n += 1
                g.wait_ge(s_w, base + n * 16)
                wgate()
                g.dma_start(wg_sc[k * 128:(k + 1) * 128, :], wgb[k % 2][:]).then_inc(s_w, 16)
                n += 1
            for k in range(16):
                wgate()
                inst = g.dma_start(wo_r[:, k * D:(k + 1) * D],
                                   wo_h[k * 128:(k + 1) * 128, :])
                if k < 15:
                    inst.then_inc(s_w, 16)
                    n += 1
            fin(inst, key)
            for c in range(NCH):
                for k in range(8):
                    key = ("WV", c, k)
                    waits(g, key)
                    fin(g.dma_start(wgb[(c * 16 + k) % WGB_N][:],
                                    wv_sc[k * 128:(k + 1) * 128, :]), key)
                for k in range(8):
                    key = ("WG", c, k)
                    waits(g, key)
                    fin(g.dma_start(wgb[(c * 16 + 8 + k) % WGB_N][:],
                                    wg_sc[k * 128:(k + 1) * 128, :]), key)


        def bc_heads(handle, width, offset=0):
            return bass.AP(handle, offset, [[DK, 128], [0, H], [1, width]])

        def bc_heads8(handle, width, offset=0):
            return bass.AP(handle, offset, [[DK, 128], [0, 8], [1, width]])

        def bc4(handle):
            return bass.AP(handle, 0, [[C, 128], [0, 4], [1, C]])

        OBOFF = (0, 512, 1536)

        def ob_off(c, g):
            return OBOFF[(4 * c + g) % 3]

        def ob_ap(c, g):
            o = ob_off(c, g)
            return pbig[:, o:o + 512]

        MAXN = -3.0e38
        MINP = 3.0e38

        @block.vector
        def _(rvec):
            vvc = [0]
            vec = _SelfSync(rvec, s_vv, vvc)

            def pad():
                rvec.wait_ge(s_vv, vvc[0])

            def vfin(inst, key):
                rvec.wait_ge(s_vv, vvc[0])
                return fin(rvec.nop(), key)

            for kf in range(8):
                key = ("WQZV", kf)
                waits(vec, key)
                vec.reduce_max(cols[:, 9:10], p1[:], axis=AX.X, apply_absolute_value=True)
                pad()
                vec.tensor_scalar(cols[:, 10:11], cols[:, 9:10], EPS_Q, MINP, ALU.max, ALU.min)
                pad()
                vec.tensor_scalar(cols[:, 12 + kf:13 + kf], cols[:, 10:11], 1.0 / QMAX, MAXN, ALU.mult, ALU.max)
                vec.reciprocal(cols[:, 11:12], cols[:, 10:11])
                pad()
                vec.tensor_scalar(cols[:, 21:22], cols[:, 11:12], QMAX, MAXN, ALU.mult, ALU.max)
                pad()
                vec.tensor_scalar_mul(cent[:], p1[:], cols[:, 21:22])
                pad()
                vec.tensor_copy(p3[:], cent[:, 0:512])
                pad()
                vec.tensor_copy(cent[:, 0:512], p3[:])
                pad()
                vec.tensor_copy(p3[:], cent[:, 512:1024])
                pad()
                vec.tensor_copy(cent[:, 512:1024], p3[:])
                pad()
                vec.tensor_scalar_mul(p1[:], cent[:], cols[:, 12 + kf:13 + kf])
                pad()
                vfin(vec.tensor_copy(xq16[:], p1[:]), key)
                key = ("WQZS", kf)
                waits(vec, key)
                inst = None
                for bb in range(8):
                    inst = vec.tensor_copy(wqT[:, bb * D + kf * 128:bb * D + kf * 128 + 128],
                                           ptall[:, bb * 128:(bb + 1) * 128])
                vfin(inst, key)
            key = ("VINIT",)
            waits(vec, key)
            vec.memset(stbuf[0][:], 0.0)
            vec.memset(stbuf[1][:], 0.0)
            vec.memset(st_hi[:], 0.0)
            vec.memset(st_lo[:], 0.0)
            inst = None
            for h in range(H):
                inst = vec.memset(vaug[:, h * 129 + 128:h * 129 + 129], 1.0)
            vfin(inst, key)

            def emit_vres(cv):
                xcv = xb[cv % 2]
                cbv = centb[cv % 2]
                key = ("VRES", cv)
                waits(vec, key)
                vec.tensor_tensor(cbv[:, 0:512], xcv[:, 0:512], pq[:, 0:512], ALU.add)
                vfin(vec.tensor_tensor(cbv[:, 512:1024], xcv[:, 512:1024],
                                       pq[:, 512:1024], ALU.add), key)

            def emit_vfin(cv):
                xcv = xb[cv % 2]
                cbv = centb[cv % 2]
                ppv = p1b[cv % 2]
                OV = 24 + 12 * (cv % 2)
                key = ("VFIN", cv)
                waits(vec, key)
                vec.scalar_tensor_tensor(cols[:, OV + 8:OV + 9], cols[:, OV + 5:OV + 6], 1.0,
                                         cols[:, OV + 11:OV + 12], ALU.mult, ALU.mult)
                pad()
                vec.tensor_scalar(cols[:, OV + 9:OV + 10], cols[:, OV + 8:OV + 9],
                                  EPS_Q, MINP, ALU.max, ALU.min)
                pad()
                vec.tensor_scalar(cols[:, OV + 10:OV + 11], cols[:, OV + 9:OV + 10],
                                  1.0 / QMAX, MAXN, ALU.mult, ALU.max)
                pad()
                vec.tensor_scalar_mul(ppv[:, 0:512], pbig[:, 0:512], cols[:, OV + 10:OV + 11])
                vec.tensor_scalar_mul(ppv[:, 512:1024], pbig[:, 512:1024], cols[:, OV + 10:OV + 11])
                pad()
                vfin(vec.tensor_add(xcv[:], cbv[:], ppv[:]), key)

            def emit_vln(cv):
                xcv = xb[cv % 2]
                cbv = centb[cv % 2]
                ppv = p1b[cv % 2]
                OV = 24 + 12 * (cv % 2)
                key = ("VLN", cv)
                waits(vec, key)
                vec.reduce_sum(cols[:, OV + 0:OV + 1], xcv[:], axis=AX.X)
                pad()
                vec.tensor_scalar(cols[:, OV + 1:OV + 2], cols[:, OV + 0:OV + 1],
                                  -1.0 / D, MAXN, ALU.mult, ALU.max)
                pad()
                vec.tensor_scalar_add(cbv[:], xcv[:], cols[:, OV + 1:OV + 2])
                pad()
                vec.scalar_tensor_tensor(ppv[:], cbv[:], 1.0, cbv[:], ALU.mult, ALU.mult)
                pad()
                vec.reduce_sum(cols[:, OV + 2:OV + 3], ppv[:], axis=AX.X)
                pad()
                vec.tensor_scalar(cols[:, OV + 3:OV + 4], cols[:, OV + 2:OV + 3],
                                  1.0 / D, EPS_LN, ALU.mult, ALU.add)
                pad()
                vfin(vec.reciprocal(cols[:, OV + 4:OV + 5], cols[:, OV + 3:OV + 4]), key)

            def emit_vq(cv):
                cbv = centb[cv % 2]
                ppv = p1b[cv % 2]
                OV = 24 + 12 * (cv % 2)
                key = ("VQ", cv)
                waits(vec, key)
                vec.reduce_max(cols[:, OV + 5:OV + 6], cbv[:], axis=AX.X, apply_absolute_value=True)
                pad()
                vec.reciprocal(cols[:, OV + 6:OV + 7], cols[:, OV + 5:OV + 6])
                pad()
                vec.tensor_scalar(cols[:, OV + 7:OV + 8], cols[:, OV + 6:OV + 7],
                                  QMAX, MAXN, ALU.mult, ALU.max)
                pad()
                vec.tensor_scalar_mul(ppv[:], cbv[:], cols[:, OV + 7:OV + 8])
                pad()
                vec.tensor_copy(p3[:], ppv[:, 0:512])
                pad()
                vec.tensor_copy(ppv[:, 0:512], p3[:])
                pad()
                vec.tensor_copy(p3[:], ppv[:, 512:1024])
                pad()
                vec.tensor_copy(ppv[:, 512:1024], p3[:])
                pad()
                vfin(vec.tensor_copy(xq16[:], ppv[:]), key)

            def emit_vstu(cv, pr):
                scv = stbuf[cv % 2]
                sdv = stbuf[(cv + 1) % 2]
                key = ("VSTU", cv, pr)
                waits(vec, key)
                inst = None
                for hh in range(2):
                    h = pr * 2 + hh
                    inst = vec.scalar_tensor_tensor(
                        sdv[0:64, h * 129:(h + 1) * 129],
                        scv[0:64, h * 129:(h + 1) * 129], float(gamC[h]),
                        pq[0:64, hh * 129:(hh + 1) * 129],
                        ALU.mult, ALU.add)
                vfin(inst, key)

            def emit_vsth(cv):
                scv = stbuf[cv % 2]
                sdv = stbuf[(cv + 1) % 2]
                key = ("VSTH", cv)
                waits(vec, key)
                vec.tensor_copy(st_hi[:], sdv[:])
                pad()
                vec.tensor_copy(scv[:], st_hi[:])
                pad()
                vfin(vec.tensor_sub(st_lo[:], sdv[:], scv[:]), key)

            emit_vln(0)
            emit_vq(0)
            for c in range(NCH):
                xcur = xb[c % 2]
                cb = centb[c % 2]
                pp = p1b[c % 2]
                CO = 24 + 12 * (c % 2)
                cosc, sinc = cosb[c % 2], sinb[c % 2]
                for which, key_name in [(0, "VROPEQ"), (1, "VROPEK")]:
                    key = (key_name, c)
                    waits(vec, key)
                    psrc = pq if which == 0 else pbig
                    for hf in range(2):
                        sl = slice(hf * 512, (hf + 1) * 512)
                        pq3 = psrc[:, sl].rearrange("p (h t) -> p h t", t=DK)
                        p13 = pp[:, sl].rearrange("p (h t) -> p h t", t=DK)
                        c3 = cb[:, sl].rearrange("p (h t) -> p h t", t=DK)
                        vec.tensor_tensor(p13[:, :, 0:32], pq3[:, :, 32:64],
                                          bc_heads8(sinc, 32, 0), ALU.mult)
                        vec.tensor_tensor(p13[:, :, 32:64], pq3[:, :, 0:32],
                                          bc_heads8(sinc, 32, 32), ALU.mult)
                        vec.tensor_tensor(c3, pq3, bc_heads8(cosc, DK, 0), ALU.mult)
                    pad()
                    vec.tensor_add(silu_g[:, 0:1024], pp[:], cb[:])
                    pad()
                    if which == 0:
                        inst = None
                        for h in range(H):
                            inst = vec.tensor_scalar_mul(
                                q16[:, h * DK:(h + 1) * DK],
                                silu_g[:, h * DK:(h + 1) * DK], qg[:, h:h + 1])
                        vfin(inst, key)
                    else:
                        for h in range(H):
                            vec.tensor_scalar_mul(
                                k16[:, h * DK:(h + 1) * DK],
                                silu_g[:, h * DK:(h + 1) * DK], kg2[:, h:h + 1])
                        inst = None
                        for h in range(H):
                            inst = vec.tensor_scalar_mul(
                                krgb[c % 2][:, h * DK:(h + 1) * DK],
                                silu_g[:, h * DK:(h + 1) * DK], kg[:, h:h + 1])
                        vfin(inst, key)
                if c >= 1:
                    for pr in range(8):
                        emit_vstu(c - 1, pr)
                    emit_vsth(c - 1)
                key = ("VEVV", c)
                waits(vec, key)
                inst = None
                for h in range(H):
                    inst = vec.tensor_copy(vaug[:, h * 129:h * 129 + 128],
                                           pbig[:, h * 128:(h + 1) * 128])
                vfin(inst, key)
                if DBG:
                    key = ("VGD", c)
                    waits(vec, key)
                    vfin(vec.tensor_copy(gdump[:], pbig[:, 0:128]), key)
                def emit_vmask(g):
                    key = ("VMASK", c, g)
                    waits(vec, key)
                    vfin(vec.tensor_tensor(
                        mskb[g % 2][:].rearrange("p (h t) -> p h t", t=128),
                        psc[:].rearrange("p (h t) -> p h t", t=128),
                        bc4(causal), ALU.mult), key)

                def emit_vcomb(g):
                    key = ("VCOMB", c, g)
                    waits(vec, key)
                    G = 56 * (g % 2)
                    rsl = r_t[:, c * H + g * 4:c * H + g * 4 + 4]
                    risl = rinv_t[:, c * H + g * 4:c * H + g * 4 + 4]
                    vec.tensor_copy(colg[:, G + 12:G + 16],
                                    pbig[:, SIG0 + g * 4:SIG0 + g * 4 + 4])
                    pad()
                    vec.scalar_tensor_tensor(colg[:, G + 16:G + 20], colg[:, G + 12:G + 16],
                                             1.0, rsl, ALU.mult, ALU.mult)
                    vec.tensor_scalar(colg[:, G + 40:G + 44], colg[:, G + 0:G + 4],
                                      -1.0 / DV, MAXN, ALU.mult, ALU.max)
                    vec.tensor_scalar(colg[:, G + 48:G + 52], colg[:, G + 44:G + 48],
                                      1.0 / DV, MAXN, ALU.mult, ALU.max)
                    pad()
                    vec.tensor_scalar(colg[:, G + 20:G + 24], colg[:, G + 16:G + 20],
                                      -1.0, MAXN, ALU.mult, ALU.max)
                    vec.scalar_tensor_tensor(colg[:, G + 4:G + 8], colg[:, G + 40:G + 44],
                                             1.0, colg[:, G + 40:G + 44], ALU.mult, ALU.mult)
                    pad()
                    vec.scalar_tensor_tensor(colg[:, G + 24:G + 28], colg[:, G + 16:G + 20],
                                             1.0, colg[:, G + 20:G + 24], ALU.mult, ALU.max)
                    vec.scalar_tensor_tensor(colg[:, G + 52:G + 56], colg[:, G + 48:G + 52],
                                             1.0, colg[:, G + 4:G + 8], ALU.mult, ALU.subtract)
                    pad()
                    vec.tensor_scalar(colg[:, G + 28:G + 32], colg[:, G + 24:G + 28],
                                      1.0, MINP, ALU.max, ALU.min)
                    vec.tensor_scalar(colg[:, G + 44:G + 48], colg[:, G + 52:G + 56],
                                      0.0, MINP, ALU.max, ALU.min)
                    pad()
                    vec.scalar_tensor_tensor(colg[:, G + 32:G + 36], colg[:, G + 28:G + 32],
                                             1.0, risl, ALU.mult, ALU.mult)
                    pad()
                    vec.scalar_tensor_tensor(colg[:, G + 36:G + 40], colg[:, G + 32:G + 36],
                                             EPS_LN, colg[:, G + 32:G + 36], ALU.mult, ALU.mult)
                    pad()
                    vec.scalar_tensor_tensor(colg[:, G + 48:G + 52], colg[:, G + 44:G + 48],
                                             1.0, colg[:, G + 36:G + 40], ALU.mult, ALU.add)
                    pad()
                    vfin(vec.reciprocal(colg[:, G + 52:G + 56], colg[:, G + 48:G + 52]), key)

                def emit_vpre(g):
                    key = ("VPRE", c, g)
                    waits(vec, key)
                    G = 56 * (g % 2)
                    inst = None
                    for hh in range(4):
                        sl = slice(hh * 128, (hh + 1) * 128)
                        inst = vec.scalar_tensor_tensor(
                            prescr2b[g % 2][:, sl], prescrb[g % 2][:, sl],
                            colg[:, G + 20 + hh:G + 21 + hh],
                            silu_g[:, g * 512 + hh * 128:g * 512 + (hh + 1) * 128],
                            ALU.add, ALU.mult)
                    vfin(inst, key)

                emit_vmask(0)
                emit_vmask(1)
                emit_vcomb(0)
                emit_vmask(2)
                emit_vcomb(1)
                emit_vpre(0)
                emit_vmask(3)
                emit_vcomb(2)
                emit_vpre(1)
                emit_vcomb(3)
                emit_vpre(2)
                emit_vpre(3)
                if c + 1 < NCH:
                    emit_vln(c + 1)
                    emit_vq(c + 1)
                emit_vres(c)
                emit_vfin(c)

        @block.scalar
        def _(rsc):
            ssc = [0]
            sc = _SelfSync(rsc, s_ss, ssc)

            def spad():
                rsc.wait_ge(s_ss, ssc[0])

            sc.memzero(padc[:, 1:2])
            sc.memzero(padc[:, 3:4])
            spad()

            def sfin(inst, key):
                rsc.wait_ge(s_ss, ssc[0])
                return fin(rsc.nop(), key)

            def emit_sevpre(cv, g):
                key = ("SEVPRE", cv, g)
                waits(sc, key)
                sfin(sc.copy(preT[:, g * 512:(g + 1) * 512],
                             ptall[:, (g % 2) * 512:(g % 2) * 512 + 512]), key)

            for c in range(NCH):
                CO = 24 + 12 * (c % 2)
                key = ("SXN", c)
                waits(sc, key)
                if c == 0:
                    sc.activation(padc[:, 1:2], padc[:, 3:4], AF.Sqrt)
                    spad()
                sc.activation(cols[:, CO + 11:CO + 12], cols[:, CO + 4:CO + 5], AF.Sqrt)
                spad()
                spad()
                sfin(sc.activation(xn16[:], centb[c % 2][:], AF.Copy,
                                   scale=cols[:, CO + 11:CO + 12]), key)
                key = ("SEVXNT", c)
                waits(sc, key)
                sc.copy(xnT[:, 0:512], ptall[:, 0:512])
                sfin(sc.copy(xnT[:, 512:1024], ptall[:, 512:1024]), key)
                key = ("SEVXQ", c)
                waits(sc, key)
                sc.copy(xqT[:, 0:512], ptall[:, 0:512])
                sfin(sc.copy(xqT[:, 512:1024], ptall[:, 512:1024]), key)
                for r_ in range(2):
                    key = ("SEVQ", c, r_)
                    waits(sc, key)
                    sfin(sc.copy(qT[0:64, r_ * 1024:(r_ + 1) * 1024], ptall[0:64, 0:1024]), key)
                for r_ in range(2):
                    key = ("SEVK", c, r_)
                    waits(sc, key)
                    sfin(sc.copy(kT[0:64, r_ * 1024:(r_ + 1) * 1024], ptall[0:64, 0:1024]), key)
                key = ("SSILU", c)
                waits(sc, key)
                sc.activation(padc[:, 1:2], padc[:, 3:4], AF.Silu)
                spad()
                for nb in range(3):
                    sc.activation(silu_g[:, nb * 512:(nb + 1) * 512],
                                  pbig[:, nb * 512:(nb + 1) * 512], AF.Silu)
                sfin(sc.activation(silu_g[:, 1536:2048], pbig[:, 1536:2048], AF.Silu), key)
                def emit_ssq(g):
                    key = ("SSQ", c, g)
                    waits(sc, key)
                    G = 56 * (g % 2)
                    junk = centb[(c + 1) % 2]
                    for hh in range(4):
                        sc.activation(junk[:, hh * 128:(hh + 1) * 128],
                                      ob_ap(c, g)[:, hh * 128:(hh + 1) * 128], AF.Copy,
                                      accum_out=colg[:, G + hh:G + hh + 1])
                    spad()
                    inst = None
                    for hh in range(4):
                        inst = sc.activation(junk[:, hh * 128:(hh + 1) * 128],
                                             ob_ap(c, g)[:, hh * 128:(hh + 1) * 128], AF.Square,
                                             accum_out=colg[:, G + 44 + hh:G + 45 + hh])
                    sfin(inst, key)

                def emit_sapply(g):
                    key = ("SAPPLY", c, g)
                    waits(sc, key)
                    G = 56 * (g % 2)
                    if c == 0 and g == 0:
                        sc.activation(padc[:, 1:2], padc[:, 3:4], AF.Sqrt)
                        spad()
                    sc.activation(colg[:, G + 8:G + 12], colg[:, G + 52:G + 56], AF.Sqrt)
                    spad()
                    for hh in range(4):
                        sc.activation(colg[:, G + 20 + hh:G + 21 + hh],
                                      colg[:, G + 40 + hh:G + 41 + hh],
                                      AF.Copy, scale=colg[:, G + 8 + hh:G + 9 + hh])
                    inst = None
                    for hh in range(4):
                        inst = sc.activation(prescrb[g % 2][:, hh * 128:(hh + 1) * 128],
                                             ob_ap(c, g)[:, hh * 128:(hh + 1) * 128], AF.Copy,
                                             scale=colg[:, G + 8 + hh:G + 9 + hh])
                    sfin(inst, key)

                emit_ssq(0)
                emit_ssq(1)
                emit_sapply(0)
                emit_ssq(2)
                emit_sapply(1)
                emit_ssq(3)
                emit_sevpre(c, 0)
                emit_sapply(2)
                emit_sevpre(c, 1)
                emit_sapply(3)
                emit_sevpre(c, 2)
                emit_sevpre(c, 3)

        @block.tensor
        def _(t):
            for kf in range(8):
                key = ("WQZT", kf)
                waits(t, key)
                inst = None
                for b in range(8):
                    inst = t.transpose(ptall[:, b * 128:(b + 1) * 128],
                                       xq16[:, b * 128:(b + 1) * 128], ident[:])
                fin(inst, key)
            def emit_txn(cv):
                key = ("TXN", cv)
                waits(t, key)
                inst = None
                for b in range(8):
                    inst = t.transpose(ptall[:, b * 128:(b + 1) * 128],
                                       xn16[:, b * 128:(b + 1) * 128], ident[:])
                fin(inst, key)

            def emit_txq(cv):
                key = ("TXQ", cv)
                waits(t, key)
                inst = None
                for b in range(8):
                    inst = t.transpose(ptall[:, b * 128:(b + 1) * 128],
                                       xq16[:, b * 128:(b + 1) * 128], ident[:])
                fin(inst, key)

            emit_txn(0)
            emit_txq(0)
            for c in range(NCH):
                for key_name, w_s, bias_off, pdst in [("TQ", wq, 0, pq), ("TK", wk, D, pbig)]:
                    key = (key_name, c)
                    waits(t, key)
                    inst = None
                    for n in range(2):
                        for k in range(8):
                            t.matmul(pdst[:, n * 512:(n + 1) * 512],
                                     xnT[:, k * 128:(k + 1) * 128],
                                     w_s[:, k * D + n * 512:k * D + (n + 1) * 512],
                                     start=(k == 0), stop=False)
                        inst = t.matmul(pdst[:, n * 512:(n + 1) * 512], ones_row[0:1, :],
                                        biasr[0:1, bias_off + n * 512:bias_off + (n + 1) * 512],
                                        start=False, stop=True)
                    fin(inst, key)
                if c >= 1:
                    emit_tstu(c - 1, 0)
                for src16, tname in [(q16, "TTRQ"), (k16, "TTRK")]:
                    for r_ in range(2):
                        key = (tname, c, r_)
                        waits(t, key)
                        inst = None
                        for hh in range(8):
                            h = r_ * 8 + hh
                            inst = t.transpose(ptall[0:64, hh * 128:(hh + 1) * 128],
                                               src16[:, h * DK:(h + 1) * DK], ident[:])
                        fin(inst, key)
                for k in range(8):
                    if c >= 1 and k < 7:
                        emit_tstu(c - 1, k + 1)
                    key = ("TV", c, k)
                    waits(t, key)
                    inst = None
                    for n in range(4):
                        inst = t.matmul(pbig[:, n * 512:(n + 1) * 512],
                                        xnT[:, k * 128:(k + 1) * 128],
                                        wgb[(c * 16 + k) % WGB_N][:, n * 512:(n + 1) * 512],
                                        start=(k == 0), stop=False)
                    if k == 7:
                        for n in range(4):
                            inst = t.matmul(
                                pbig[:, n * 512:(n + 1) * 512], ones_row[0:1, :],
                                biasr[0:1, 2 * D + n * 512:2 * D + (n + 1) * 512],
                                start=False, stop=True)
                    fin(inst, key)
                for k in range(8):
                    key = ("TG", c, k)
                    waits(t, key)
                    inst = None
                    for n in range(4):
                        inst = t.matmul(pbig[:, n * 512:(n + 1) * 512],
                                        xnT[:, k * 128:(k + 1) * 128],
                                        wgb[(c * 16 + 8 + k) % WGB_N][:, n * 512:(n + 1) * 512],
                                        start=(k == 0), stop=False)
                    if k == 7:
                        for n in range(4):
                            inst = t.matmul(
                                pbig[:, n * 512:(n + 1) * 512], ones_row[0:1, :],
                                biasr[0:1, 2 * D + DD + n * 512:2 * D + DD + (n + 1) * 512],
                                start=False, stop=True)
                    fin(inst, key)
                def emit_ttrp(cv, g):
                    key = ("TTRP", cv, g)
                    waits(t, key)
                    inst = None
                    for hh in range(4):
                        inst = t.transpose(
                            ptall[:, (g % 2) * 512 + hh * 128:(g % 2) * 512 + (hh + 1) * 128],
                            prescr2b[g % 2][:, hh * 128:(hh + 1) * 128], ident[:])
                    fin(inst, key)

                def emit_tstu(cv, pr):
                    key = ("TSTU", cv, pr)
                    waits(t, key)
                    inst = None
                    for hh in range(2):
                        h = pr * 2 + hh
                        inst = t.matmul(pq[0:64, hh * 129:(hh + 1) * 129],
                                        krgb[cv % 2][:, h * DK:(h + 1) * DK],
                                        vaug[:, h * 129:(h + 1) * 129],
                                        start=True, stop=True)
                    fin(inst, key)

                for g in range(NG):
                    key = ("TSC", c, g)
                    waits(t, key)
                    inst = None
                    for hh in range(4):
                        h = g * 4 + hh
                        inst = t.matmul(psc[:, hh * 128:(hh + 1) * 128],
                                        kT[0:64, h * 128:(h + 1) * 128],
                                        qT[0:64, h * 128:(h + 1) * 128],
                                        start=True, stop=True)
                    fin(inst, key)
                    key = ("TOUT", c, g)
                    waits(t, key)
                    obo = ob_off(c, g)
                    msrc = mskb[g % 2]
                    inst = None
                    for hh in range(4):
                        h = g * 4 + hh
                        t.matmul(pbig[:, obo + hh * 128:obo + (hh + 1) * 128],
                                 msrc[:, hh * 128:(hh + 1) * 128],
                                 vaug[:, h * 129:h * 129 + 128], start=True, stop=False)
                        t.matmul(pbig[:, SIG0 + h:SIG0 + h + 1],
                                 msrc[:, hh * 128:(hh + 1) * 128],
                                 ones_col[:, 0:1], start=True, stop=False)
                        qslice = qT[0:64, h * 128:(h + 1) * 128]
                        t.matmul(pbig[:, obo + hh * 128:obo + (hh + 1) * 128],
                                 qslice, st_hi[0:64, h * 129:h * 129 + 128],
                                 start=False, stop=False)
                        t.matmul(pbig[:, obo + hh * 128:obo + (hh + 1) * 128],
                                 qslice, st_lo[0:64, h * 129:h * 129 + 128],
                                 start=False, stop=True)
                        t.matmul(pbig[:, SIG0 + h:SIG0 + h + 1],
                                 qslice, st_hi[0:64, h * 129 + 128:h * 129 + 129],
                                 start=False, stop=False)
                        inst = t.matmul(pbig[:, SIG0 + h:SIG0 + h + 1],
                                        qslice, st_lo[0:64, h * 129 + 128:h * 129 + 129],
                                        start=False, stop=True)
                    fin(inst, key)
                emit_ttrp(c, 0)
                emit_ttrp(c, 1)
                emit_ttrp(c, 2)
                emit_ttrp(c, 3)
                key = ("TFFN", c)
                waits(t, key)
                inst = None
                for n in range(2):
                    for k in range(8):
                        inst = t.matmul(pbig[:, n * 512:(n + 1) * 512],
                                        xqT[:, k * 128:(k + 1) * 128],
                                        wqT[:, k * D + n * 512:k * D + (n + 1) * 512],
                                        start=(k == 0), stop=(k == 7))
                fin(inst, key)
                for k in range(16):
                    key = ("TWO", c, k)
                    waits(t, key)
                    inst = None
                    for n in range(2):
                        inst = t.matmul(pq[:, n * 512:(n + 1) * 512],
                                        preT[:, k * 128:(k + 1) * 128],
                                        wo_r[:, k * D + n * 512:k * D + (n + 1) * 512],
                                        start=(k == 0), stop=False)
                    if k == 15:
                        for n in range(2):
                            t.matmul(pq[:, n * 512:(n + 1) * 512], ones_row[0:1, :],
                                     biasr[0:1, 2 * D + 2 * DD + n * 512:
                                           2 * D + 2 * DD + (n + 1) * 512],
                                     start=False, stop=False)
                            inst = t.matmul(pq[:, n * 512:(n + 1) * 512], ones_row[0:1, :],
                                            biasr[0:1, 3 * D + 2 * DD + n * 512:
                                                  3 * D + 2 * DD + (n + 1) * 512],
                                            start=False, stop=True)
                    fin(inst, key)
                if c + 1 < NCH:
                    emit_txn(c + 1)
                    emit_txq(c + 1)

    return nc


def kernel(L0_attn_x, L0_attn_Wq, L0_attn_bq, L0_attn_Wk, L0_attn_bk,
           L0_attn_Wv, L0_attn_bv, L0_attn_Wg, L0_attn_bg, L0_attn_Wo,
           L0_attn_bo, L0_ffn_weight, L0_ffn_bias):
    hc = _HC
    perm = np.concatenate([np.arange(0, DK, 2), np.arange(1, DK, 2)])
    full_perm = np.concatenate([h * DK + perm for h in range(H)])
    wq_p = np.ascontiguousarray(np.asarray(L0_attn_Wq, np.float32)[:, full_perm])
    wk_p = np.ascontiguousarray(np.asarray(L0_attn_Wk, np.float32)[:, full_perm])
    bq_p = np.ascontiguousarray(np.asarray(L0_attn_bq, np.float32)[full_perm].reshape(1, D))
    bk_p = np.ascontiguousarray(np.asarray(L0_attn_bk, np.float32)[full_perm].reshape(1, D))

    common = dict(
        wq=wq_p, wk=wk_p,
        wv=np.ascontiguousarray(np.asarray(L0_attn_Wv, np.float32)),
        wg=np.ascontiguousarray(np.asarray(L0_attn_Wg, np.float32)),
        wo=np.ascontiguousarray(np.asarray(L0_attn_Wo, np.float32)),
        wf=np.ascontiguousarray(np.asarray(L0_ffn_weight, np.float32)),
        bq=bq_p, bk=bk_p,
        bv=np.asarray(L0_attn_bv, np.float32).reshape(1, DD),
        bg=np.asarray(L0_attn_bg, np.float32).reshape(1, DD),
        bo=np.asarray(L0_attn_bo, np.float32).reshape(1, D),
        bf=np.asarray(L0_ffn_bias, np.float32).reshape(1, D),
        cosC=hc["cosC"], sinC=hc["sinC"], qg=hc["qg"], kg=hc["kg"],
        kg2=hc["kg2"], causal=hc["causal"], r_t=hc["r"], rinv_t=hc["rinv"],
        ident=hc["ident"], identf=hc["identf"], ones_row=hc["ones_row"],
        ones_col=hc["ones_col"],
    )
    xs = np.asarray(L0_attn_x, np.float32)
    in_maps = [dict(common, x=np.ascontiguousarray(xs[b])) for b in range(8)]
    nc = build_nc()
    res = run_bass_kernel_spmd(nc, in_maps, core_ids=list(range(8)))
    return np.stack([np.asarray(r["y"], np.float32) for r in res.results], axis=0)
